# version 04 (final)
# baseline (speedup 1.0000x reference)
import sys
if "/opt/trn_rl_repo" not in sys.path:
    sys.path.insert(0, "/opt/trn_rl_repo")
import contextlib
import numpy as np
import ml_dtypes
import concourse.bass as bass
import concourse.mybir as mybir
from concourse.bass_utils import run_bass_kernel_spmd

F32 = mybir.dt.float32
BF16 = mybir.dt.bfloat16
U8 = mybir.dt.uint8
AF = mybir.ActivationFunctionType
ALU = mybir.AluOpType
AX = mybir.AxisListType

NCORES = 8
B, S, D = 16, 2048, 1024
BPC = B // NCORES
T = BPC * S
TC = 512
NCH = T // TC
NSTR = D // 128
RING_SLOTS = 3
KU = 1024
LN2 = float(np.log(2.0))
INV_LN2 = float(1.0 / np.log(2.0))

ALL_BLOCKS = ["l0a", "l0f", "l1a", "l1f", "l2a", "l2f", "l3a", "l3f"]

CONST_COLS = {"ones": 0, "m7ln2": 1, "m3ln2": 2, "eps1e5": 3, "one": 4,
              "half": 5, "zero": 6, "p9ln2": 7, "m9ln2": 8}
CONST_VALS = {"ones": 1.0, "m7ln2": -7.0 * LN2, "m3ln2": -3.0 * LN2,
              "eps1e5": 1e-5, "one": 1.0, "half": 0.5, "zero": 0.0,
              "p9ln2": 9.0 * LN2, "m9ln2": -9.0 * LN2}
NCONS = 12



def _prep_w(w, k_unit=KU):
    K, M = w.shape
    assert K % 128 == 0 and M % 128 == 0, (K, M)
    n_m, n_k = M // 128, K // 128
    n_ku = -(-K // k_unit)
    a = w.reshape(n_k, 128, n_m, 128).transpose(2, 1, 0, 3)
    a = np.ascontiguousarray(a).reshape(n_m, 128, n_k * 128)
    if n_k * 128 < n_ku * k_unit:
        pad = np.zeros((n_m, 128, n_ku * k_unit - n_k * 128), dtype=w.dtype)
        a = np.concatenate([a, pad], axis=2)
    a = a.reshape(n_m, 128, n_ku, k_unit).transpose(0, 2, 1, 3)
    return np.ascontiguousarray(a)


def _col128(v):
    v = np.asarray(v, dtype=np.float32)
    n = v.shape[0] // 128
    return np.ascontiguousarray(v.reshape(n, 128).T)



class Sem:
    __slots__ = ("h", "n")

    def __init__(self, h):
        self.h, self.n = h, 0

    def inc(self, by=1):
        self.n += by
        return self.n


ENGS = ("t", "v", "s", "g", "y")


class Prog:
    def __init__(self):
        self.q = {e: [] for e in ENGS}
        self.waited = {e: {} for e in ENGS}

    def op(self, e, f=None, wait=(), inc=()):
        w = []
        for s_, v_ in wait:
            if v_ is None:
                if self.waited[e].get(id(s_), -1) < 10 ** 9:
                    self.waited[e][id(s_)] = 10 ** 9
                    w.append((s_, None))
            elif v_ > 0 and self.waited[e].get(id(s_), -1) < v_:
                self.waited[e][id(s_)] = v_
                w.append((s_, v_))
        assert f is not None or not inc
        for s_, b_ in inc:
            s_.inc(b_)

        dr = e in ("v", "s", "g")

        def run(eng, f=f, w=tuple(w), inc=tuple(inc), dr=dr):
            for s_, v_ in w:
                eng.wait_ge(s_.h, s_.n if v_ is None else v_)
            if f is not None:
                if dr:
                    eng.drain()
                ins = f(eng)
                for j, (s_, b_) in enumerate(inc):
                    if j == 0:
                        ins = ins.then_inc(s_.h, b_)
                    else:
                        eng.drain().then_inc(s_.h, b_)
        self.q[e].append(run)


class Ctx:
    pass


def psum_acquire(cx):
    b = cx.banks[cx.bank_rr % len(cx.banks)]
    cx.bank_rr += 1
    return b


def bwait(b):
    return [b["guard"]] if b["guard"] else []


def stream_unit(cx, src_ap):
    sid = cx.slab_id
    cx.slab_id += 1
    j = sid % RING_SLOTS
    slot = cx.ring[:, j * KU:(j + 1) * KU]
    need = sid - (RING_SLOTS - 1)
    cx.slot_uses[j] += 1
    cx.P.op("y", lambda e, s=src_ap, d=slot: e.dma_start(out=d, in_=s),
            wait=([(cx.w_free, need)] if need > 0 else []),
            inc=[(cx.wslot[j], 16)])
    return slot, (cx.wslot[j], 16 * cx.slot_uses[j])


def emit_linear(cx, w_dram, K, M, rhs_strips, Tc, evac):
    P = cx.P
    n_m, n_k = M // 128, K // 128
    kt_per_u = KU // 128
    n_ku = -(-n_k // kt_per_u)
    for m in range(n_m):
        b = psum_acquire(cx)
        bw = bwait(b)
        first = True
        for ku in range(n_ku):
            slot, wld = stream_unit(cx, w_dram[m, ku])
            for kk in range(kt_per_u):
                k = ku * kt_per_u + kk
                if k >= n_k:
                    break
                last = k == n_k - 1
                lhsT = slot[:, kk * 128:(kk + 1) * 128]
                wait = ([wld] if kk == 0 else []) + (bw if first else [])
                inc = []
                if kk == kt_per_u - 1 or last:
                    inc.append((cx.w_free, 1))
                if last:
                    inc.append((cx.TT, 1))
                P.op("t", lambda e, o=b["h"][:, :Tc], l=lhsT, r=rhs_strips[k],
                     st=first, sp=last: e.matmul(out=o, lhsT=l, rhs=r,
                                                 start=st, stop=sp),
                     wait=wait, inc=inc)
                first = False
        evac(m, b, (cx.TT, cx.TT.n))


def load_x_chunk(cx, c, dst=None):
    dst = dst if dst is not None else cx.xbuf
    src = cx.xsrc[:, c * TC:(c + 1) * TC].rearrange("(s p) c -> p s c", p=128)
    dv = dst[:, 0:NSTR * TC].rearrange("p (s c) -> p s c", c=TC)
    prev = cx.LX.n
    cx.P.op("g", lambda e, s=src, d=dv: e.dma_start(out=d, in_=s),
            wait=([(cx.LX, prev)] if prev else []), inc=[(cx.LX, 16)])
    return (cx.LX, cx.LX.n)


def store_x_chunk(cx, c, src_buf, dst_dram, wait):
    dst = dst_dram[:, c * TC:(c + 1) * TC].rearrange("(s p) c -> p s c", p=128)
    sv = src_buf[:, 0:NSTR * TC].rearrange("p (s c) -> p s c", c=TC)
    prev = cx.LX.n
    cx.P.op("g", lambda e, s=sv, d=dst: e.dma_start(out=d, in_=s),
            wait=list(wait) + ([(cx.LX, prev)] if prev else []),
            inc=[(cx.LX, 16)])
    return (cx.LX, cx.LX.n)


def open_const_group(cx, name):
    h = cx.es.enter_context(cx.nc.semaphore(name))
    cx.cgroup = Sem(h)
    return cx.cgroup


def const_dma(cx, f):
    cx.P.op("g", f, inc=[(cx.cgroup, 16)])
    return (cx.cgroup, None)


def emit_pmax_row(cx, ta, tb, Tc):
    P = cx.P
    P.op("v", lambda e: e.engine_nop(), inc=[(cx.VV, 1)])
    for hi, w_ in ((64, 64), (32, 32)):
        prev = cx.LX.n
        P.op("g", lambda e, s=ta[hi:hi + w_, 0:Tc], d=tb[0:w_, 0:Tc]:
             e.dma_start(out=d, in_=s),
             wait=[(cx.VV, cx.VV.n)] + ([(cx.LX, prev)] if prev else []),
             inc=[(cx.LX, 16)])
        P.op("v", lambda e, w2=w_: e.tensor_tensor(
            out=ta[0:w2, 0:Tc], in0=ta[0:w2, 0:Tc], in1=tb[0:w2, 0:Tc],
            op=ALU.max), wait=[(cx.LX, cx.LX.n)], inc=[(cx.VV, 1)])
    for sh in (16, 8, 4, 2, 1):
        mask = [(i + sh) % 32 for i in range(32)]
        P.op("v", lambda e, m_=mask: e.stream_shuffle(
            out=tb[0:32, 0:Tc], in_=ta[0:32, 0:Tc], mask=m_))
        P.op("v", lambda e: e.tensor_tensor(
            out=ta[0:32, 0:Tc], in0=ta[0:32, 0:Tc], in1=tb[0:32, 0:Tc],
            op=ALU.max))


def emit_ln(cx, x_strips, out_strips, Tc, nstr, gam=None, bet=None, eps=1e-5):
    P = cx.P
    inv_n = 1.0 / (nstr * 128)
    bs = psum_acquire(cx)
    bq = psum_acquire(cx)
    bsw, bqw = bwait(bs), bwait(bq)
    for i, xs in enumerate(x_strips):
        P.op("t", lambda e, o=bs["h"][0:1, :Tc], r=xs, st=(i == 0),
             sp=(i == nstr - 1): e.matmul(out=o, lhsT=cx.ones_col, rhs=r,
                                          start=st, stop=sp),
             wait=(bsw if i == 0 else []) + cx.const_wait,
             inc=([(cx.TT, 1)] if i == nstr - 1 else []))
    for i, xs in enumerate(x_strips):
        sqv = cx.sq[:, (i % 2) * Tc:(i % 2) * Tc + Tc]
        wait_s = [(cx.TT, cx.sq_guard[i % 2])] if cx.sq_guard[i % 2] else []
        P.op("s", lambda e, o=sqv, r=xs: e.activation(out=o, in_=r, func=AF.Square),
             wait=wait_s, inc=[(cx.SS, 1)])
        P.op("t", lambda e, o=bq["h"][0:1, :Tc], r=sqv, st=(i == 0),
             sp=(i == nstr - 1): e.matmul(out=o, lhsT=cx.ones_col, rhs=r,
                                          start=st, stop=sp),
             wait=[(cx.SS, cx.SS.n)] + (bqw if i == 0 else []), inc=[(cx.TT, 1)])
        cx.sq_guard[i % 2] = cx.TT.n
    st_ = cx.stats
    mu = st_[0:1, 0 * TC:0 * TC + Tc]
    m2 = st_[0:1, 1 * TC:1 * TC + Tc]
    rs = st_[0:1, 2 * TC:2 * TC + Tc]
    Bw = st_[0:1, 3 * TC:3 * TC + Tc]
    P.op("v", lambda e: e.tensor_scalar_mul(out=mu, in0=bs["h"][0:1, :Tc], scalar1=inv_n),
         wait=[(cx.TT, cx.TT.n)])
    P.op("v", lambda e: e.tensor_scalar_mul(out=m2, in0=bq["h"][0:1, :Tc], scalar1=inv_n),
         inc=[(cx.VV, 1)])
    bs["guard"] = (cx.VV, cx.VV.n)
    bq["guard"] = (cx.VV, cx.VV.n)
    P.op("v", lambda e: e.scalar_tensor_tensor(out=rs, in0=mu, scalar=-1.0,
                                               in1=mu, op0=ALU.mult, op1=ALU.mult))
    P.op("v", lambda e: e.tensor_add(out=m2, in0=m2, in1=rs), inc=[(cx.VV, 1)])
    P.op("s", lambda e: e.activation(out=rs, in_=m2, func=AF.Sqrt, bias=cx.cc("eps1e5", 1)),
         wait=[(cx.VV, cx.VV.n)], inc=[(cx.SS, 1)])
    P.op("v", lambda e: e.reciprocal(out=rs, in_=rs), wait=[(cx.SS, cx.SS.n)],
         inc=[(cx.VV, 1)])
    P.op("v", lambda e: e.scalar_tensor_tensor(out=Bw, in0=mu, scalar=-1.0,
                                               in1=rs, op0=ALU.mult, op1=ALU.mult),
         wait=[(cx.SS, cx.SS.n)], inc=[(cx.VV, 1)])
    bA = psum_acquire(cx)
    bB = psum_acquire(cx)
    P.op("t", lambda e, o=bA["h"][:, :Tc]: e.matmul(out=o, lhsT=cx.onesr, rhs=rs,
                                                    start=True, stop=True),
         wait=bwait(bA) + [(cx.VV, cx.VV.n)] + cx.const_wait)
    P.op("t", lambda e, o=bB["h"][:, :Tc]: e.matmul(out=o, lhsT=cx.onesr, rhs=Bw,
                                                    start=True, stop=True),
         wait=bwait(bB), inc=[(cx.TT, 1)])
    tmm = (cx.TT, cx.TT.n)
    for i, (xs, os) in enumerate(zip(x_strips, out_strips)):
        P.op("v", lambda e, o=os, a=xs, b_=bA["h"][:, :Tc]: e.tensor_mul(
            out=o, in0=a, in1=b_), wait=[tmm])
        if gam is None:
            P.op("v", lambda e, o=os, b_=bB["h"][:, :Tc]: e.tensor_add(
                out=o, in0=o, in1=b_), inc=[(cx.VV, 1)])
        else:
            P.op("v", lambda e, o=os, b_=bB["h"][:, :Tc]: e.tensor_add(
                out=o, in0=o, in1=b_))
            P.op("v", lambda e, o=os, g_=gam[i], bt=bet[i]: e.tensor_scalar(
                out=o, in0=o, scalar1=g_, scalar2=bt, op0=ALU.mult, op1=ALU.add),
                inc=[(cx.VV, 1)])
    bA["guard"] = (cx.VV, cx.VV.n)
    bB["guard"] = (cx.VV, cx.VV.n)
    return (cx.VV, cx.VV.n)



def emit_floor(cx, r_ap, keep_ap, tmp_ap, levels, b1_out=None):
    P = cx.P
    P.op("v", lambda e, o=keep_ap, i=r_ap: e.tensor_copy(out=o, in_=i))
    for k in levels:
        P.op("v", lambda e, o=tmp_ap, i=r_ap, k_=float(k): e.tensor_scalar(
            out=o, in0=i, scalar1=k_, scalar2=k_, op0=ALU.is_ge, op1=ALU.mult))
        P.op("v", lambda e, o=r_ap, i0=r_ap, i1=tmp_ap: e.tensor_sub(
            out=o, in0=i0, in1=i1))


def emit_l3f(cx):
    P, nc = cx.P, cx.nc
    w8 = cx.win("l3f_w", [8, 1, 128, KU], U8)
    wsc = cx.win("l3f_wsc", [1, 1])
    bia = cx.win("l3f_b", [128, 8])
    last = cx.cur_block_last

    with contextlib.ExitStack() as les:
        wu8 = les.enter_context(nc.sbuf_tensor("l3f_wu8", [128, 8192], U8))
        wbf = les.enter_context(nc.sbuf_tensor("l3f_wbf", [128, 8192], BF16))
        sws = les.enter_context(nc.sbuf_tensor("l3f_sws", [1, 1], F32))
        sbi = les.enter_context(nc.sbuf_tensor("l3f_sbi", [128, 8], F32))
        qbf = les.enter_context(nc.sbuf_tensor("l3f_qbf", [128, NSTR * TC], BF16))
        ta = les.enter_context(nc.sbuf_tensor("l3f_ta", [128, NSTR * TC], F32))
        tb = les.enter_context(nc.sbuf_tensor("l3f_tb", [128, NSTR * TC], F32))
        t0 = les.enter_context(nc.sbuf_tensor("l3f_t0", [128, TC], F32))[:, :]
        t1 = les.enter_context(nc.sbuf_tensor("l3f_t1", [128, TC], F32))[:, :]
        t2 = les.enter_context(nc.sbuf_tensor("l3f_t2", [128, TC], F32))[:, :]
        scb = les.enter_context(nc.sbuf_tensor("l3f_scb", [128, TC], F32))[:, :]

        const_dma(cx, lambda e: e.dma_start(
            out=wu8[:, :].rearrange("p (m c) -> p m c", c=KU),
            in_=w8.rearrange("m ku p c -> p m (ku c)")))
        const_dma(cx, lambda e: e.dma_start(out=sws[:, :], in_=wsc))
        w_ = const_dma(cx, lambda e: e.dma_start(out=sbi[:, :], in_=bia))
        wldw = [w_]
        for part in range(8):
            sl = slice(part * 1024, (part + 1) * 1024)
            bfv = ta[:, 0:1024]
            a2 = tb[:, 0:1024]
            tt0 = ta[:, 1024:2048]
            tt1 = tb[:, 1024:2048]
            e8 = ta[:, 2048:3072]
            sv = tb[:, 2048:3072]
            g0 = ta[:, 3072:4096]
            P.op("v", lambda e, o=bfv, i=wu8[:, sl]: e.tensor_copy(out=o, in_=i),
                 wait=wldw)
            P.op("v", lambda e, o=sv, i=bfv: e.tensor_single_scalar(
                out=o, in_=i, scalar=128.0, op=ALU.is_ge))
            P.op("v", lambda e, o=a2, i0=sv, i1=bfv: e.scalar_tensor_tensor(
                out=o, in0=i0, scalar=-128.0, in1=i1, op0=ALU.mult, op1=ALU.add))
            P.op("v", lambda e, o=e8, i=a2: e.tensor_scalar_mul(
                out=o, in0=i, scalar1=0.125))
            emit_floor(cx, e8, tt0, tt1, (8, 4, 2, 1))
            P.op("v", lambda e, o=e8, i0=tt0, i1=e8: e.tensor_sub(
                out=o, in0=i0, in1=i1))
            P.op("v", lambda e, o=tt1, i0=e8: e.tensor_scalar(
                out=o, in0=i0, scalar1=8.0, scalar2=-1.0, op0=ALU.mult,
                op1=ALU.mult))
            P.op("v", lambda e, o=tt1, i0=tt1, i1=a2: e.tensor_add(
                out=o, in0=i0, in1=i1))
            P.op("v", lambda e, o=g0, i=e8: e.tensor_single_scalar(
                out=o, in_=i, scalar=1.0, op=ALU.is_ge))
            P.op("v", lambda e, o=e8, i=e8: e.tensor_scalar_max(
                out=o, in0=i, scalar1=1.0), inc=[(cx.VV, 1)])
            P.op("s", lambda e, o=e8, i=e8: e.activation(
                out=o, in_=i, func=AF.Exp, scale=LN2, bias=cx.cc("m7ln2")),
                wait=[(cx.VV, cx.VV.n)], inc=[(cx.SS, 1)])
            P.op("v", lambda e, o=tt1, i0=tt1, i1=g0: e.scalar_tensor_tensor(
                out=o, in0=i0, scalar=0.125, in1=i1, op0=ALU.mult, op1=ALU.add),
                wait=[(cx.SS, cx.SS.n)])
            P.op("v", lambda e, o=tt1, i0=tt1, i1=e8: e.tensor_mul(out=o, in0=i0, in1=i1))
            P.op("v", lambda e, o=sv, i0=sv, i1=tt1: e.tensor_mul(out=o, in0=i0, in1=i1))
            P.op("v", lambda e, o=wbf[:, sl], i0=sv, i1=tt1: e.scalar_tensor_tensor(
                out=o, in0=i0, scalar=-2.0, in1=i1, op0=ALU.mult, op1=ALU.add),
                inc=[(cx.VV, 1)])
        wdec = (cx.VV, cx.VV.n)

        xdst = cx.xdst_for(last)
        for c in range(NCH):
            lw = load_x_chunk(cx, c)
            xs = [cx.xbuf[:, i * TC:(i + 1) * TC] for i in range(NSTR)]
            xn = [cx.xnbuf[:, i * TC:(i + 1) * TC] for i in range(NSTR)]
            P.op("v", wait=[lw])
            P.op("s", wait=[lw])
            P.op("t", wait=[lw])
            lnw = emit_ln(cx, xs, xn, TC, NSTR)
            P.op("s", lambda e, a0=xn[0]: e.activation(
                out=ta[:, :TC], in_=a0, func=AF.Abs),
                wait=[lnw, (cx.VV, cx.VV.n)], inc=[(cx.SS, 1)])
            for i in range(1, NSTR):
                P.op("s", lambda e, a1=xn[i], i_=i: e.activation(
                    out=tb[:, :TC], in_=a1, func=AF.Abs),
                    wait=[(cx.VV, cx.VV.n)], inc=[(cx.SS, 1)])
                P.op("v", lambda e: e.tensor_tensor(
                    out=ta[:, :TC], in0=ta[:, :TC], in1=tb[:, :TC], op=ALU.max),
                    wait=[(cx.SS, cx.SS.n)], inc=[(cx.VV, 1)])
            emit_pmax_row(cx, ta, tb, TC)
            st_ = cx.stats
            xsc = st_[0:1, 4 * TC:5 * TC]
            xsw = st_[0:1, 5 * TC:6 * TC]
            rcp = st_[0:1, 6 * TC:7 * TC]
            P.op("v", lambda e: e.tensor_scalar(
                out=xsc, in0=ta[0:1, :TC], scalar1=1e-12, scalar2=1.0 / 448.0,
                op0=ALU.max, op1=ALU.mult))
            P.op("v", lambda e: e.reciprocal(out=rcp, in_=xsc))
            P.op("v", lambda e: e.tensor_scalar(
                out=xsw, in0=xsc, scalar1=sws[0:1, 0:1], scalar2=None,
                op0=ALU.mult), wait=wldw, inc=[(cx.VV, 1)])
            bD = psum_acquire(cx)
            P.op("t", lambda e, o=bD["h"][:, :TC]: e.matmul(
                out=o, lhsT=cx.onesr, rhs=rcp, start=True, stop=True),
                wait=bwait(bD) + [(cx.VV, cx.VV.n)], inc=[(cx.TT, 1)])
            tbD = (cx.TT, cx.TT.n)
            bW = psum_acquire(cx)
            P.op("t", lambda e, o=bW["h"][:, :TC]: e.matmul(
                out=o, lhsT=cx.onesr, rhs=xsw, start=True, stop=True),
                wait=bwait(bW), inc=[(cx.TT, 1)])
            tscw = (cx.TT, cx.TT.n)
            P.op("v", lambda e, o=scb[:, :], i=bW["h"][:, :TC]: e.tensor_copy(
                out=o, in_=i), wait=[tscw], inc=[(cx.VV, 1)])
            bW["guard"] = (cx.VV, cx.VV.n)
            for i in range(NSTR):
                P.op("v", lambda e, i_=i, bh=bD["h"]: e.tensor_tensor(
                    out=xn[i_], in0=xn[i_], in1=bh[:, :TC], op=ALU.mult),
                    wait=([tbD] if i == 0 else []),
                    inc=([(cx.VV, 1)] if i == NSTR - 1 else []))
            bD["guard"] = (cx.VV, cx.VV.n)
            for i in range(NSTR):
                A = ta[:, i * TC:(i + 1) * TC]
                Bx = tb[:, i * TC:(i + 1) * TC]
                vv = xn[i]
                P.op("s", lambda e, o=A, i_=vv: e.activation(
                    out=o, in_=i_, func=AF.Abs), wait=[(cx.VV, cx.VV.n)],
                    inc=[(cx.SS, 1)])
                P.op("v", lambda e, o=A, i_=A: e.tensor_scalar_max(
                    out=o, in0=i_, scalar1=1e-30), wait=[(cx.SS, cx.SS.n)],
                    inc=[(cx.VV, 1)])
                P.op("s", lambda e, o=Bx, i_=A: e.activation(
                    out=o, in_=i_, func=AF.Ln), wait=[(cx.VV, cx.VV.n)],
                    inc=[(cx.SS, 1)])
                P.op("v", lambda e, o=Bx, i_=Bx: e.tensor_scalar(
                    out=o, in0=i_, scalar1=INV_LN2, scalar2=-6.0,
                    op0=ALU.mult, op1=ALU.max), wait=[(cx.SS, cx.SS.n)])
                P.op("v", lambda e, o=Bx, i_=Bx: e.tensor_scalar_add(
                    out=o, in0=i_, scalar1=6.0))
                emit_floor(cx, Bx, t1, t0, (8, 4, 2, 1))
                P.op("v", lambda e, o=Bx, i0=t1, i1=Bx: e.tensor_sub(
                    out=o, in0=i0, in1=i1), inc=[(cx.VV, 1)])
                P.op("s", lambda e, o=t2, i_=Bx: e.activation(
                    out=o[:, :], in_=i_, func=AF.Exp, scale=-LN2,
                    bias=cx.cc("p9ln2")), wait=[(cx.VV, cx.VV.n)])
                P.op("s", lambda e, o=Bx, i_=Bx: e.activation(
                    out=o, in_=i_, func=AF.Exp, scale=LN2, bias=cx.cc("m9ln2")),
                    inc=[(cx.SS, 1)])
                P.op("v", lambda e, o=A, i0=vv, i1=t2: e.tensor_mul(
                    out=o, in0=i0, in1=t2[:, :]), wait=[(cx.SS, cx.SS.n)])
                P.op("v", lambda e, o=A, i_=A: e.tensor_scalar_add(
                    out=o, in0=i_, scalar1=16.0))
                emit_floor(cx, A, t1, t0, (16, 8, 4, 2, 1))
                P.op("v", lambda e, o=t1, i0=t1, i1=A: e.tensor_sub(
                    out=o[:, :], in0=i0[:, :], in1=i1))
                P.op("v", lambda e, o=t1, i0=A, i1=t1: e.scalar_tensor_tensor(
                    out=o[:, :], in0=i0, scalar=0.5, in1=i1[:, :],
                    op0=ALU.is_gt, op1=ALU.add))
                P.op("v", lambda e, o=t0, i0=A, i1=t0: e.scalar_tensor_tensor(
                    out=o[:, :], in0=i0, scalar=0.5, in1=i1[:, :],
                    op0=ALU.is_equal, op1=ALU.mult))
                P.op("v", lambda e, o=t1, i1=t0: e.tensor_add(
                    out=o[:, :], in0=o[:, :], in1=i1[:, :]))
                P.op("v", lambda e, o=t1: e.tensor_scalar_add(
                    out=o[:, :], in0=o[:, :], scalar1=-16.0))
                P.op("v", lambda e, o=qbf[:, i * TC:(i + 1) * TC], i0=t1, i1=Bx:
                     e.tensor_mul(out=o, in0=i0[:, :], in1=i1),
                     inc=[(cx.VV, 1)])
            qw = (cx.VV, cx.VV.n)

            evw = []
            for m in range(8):
                bk = psum_acquire(cx)
                for k in range(8):
                    P.op("t", lambda e, o=bk["h"][:, :TC],
                         l=wbf[:, m * KU + k * 128:m * KU + (k + 1) * 128],
                         r=qbf[:, k * TC:(k + 1) * TC], st=(k == 0), sp=(k == 7):
                         e.matmul(out=o, lhsT=l, rhs=r, start=st, stop=sp),
                         wait=(bwait(bk) if k == 0 else []) + ([qw, wdec] if k == 0 else []),
                         inc=([(cx.TT, 1)] if k == 7 else []))
                tmm = (cx.TT, cx.TT.n)
                P.op("v", lambda e, o=ta[:, m * TC:(m + 1) * TC], i0=bk["h"][:, :TC]:
                     e.tensor_mul(out=o, in0=i0, in1=scb[:, :]), wait=[tmm],
                     inc=[(cx.VV, 1)])
                bk["guard"] = (cx.VV, cx.VV.n)
                P.op("v", lambda e, o=xs[m], i0=ta[:, m * TC:(m + 1) * TC],
                     bc=sbi[:, m:m + 1], i1=xs[m]: e.scalar_tensor_tensor(
                     out=o, in0=i0, scalar=bc, in1=i1, op0=ALU.add, op1=ALU.add),
                     wait=wldw, inc=[(cx.VV, 1)])
                evw.append((cx.VV, cx.VV.n))
            store_x_chunk(cx, c, cx.xbuf, xdst, wait=evw[-1:])
    cx.xsrc = xdst




def emit_bcast_tile(cx, row_ap, dst_sb, Tc, extra_wait=()):
    P = cx.P
    b = psum_acquire(cx)
    P.op("t", lambda e, o=b["h"][:, :Tc]: e.matmul(
        out=o, lhsT=cx.onesr, rhs=row_ap, start=True, stop=True),
        wait=bwait(b) + list(extra_wait) + cx.const_wait, inc=[(cx.TT, 1)])
    P.op("v", lambda e, o=dst_sb, i=b["h"][:, :Tc]: e.tensor_copy(out=o, in_=i),
         wait=[(cx.TT, cx.TT.n)], inc=[(cx.VV, 1)])
    b["guard"] = (cx.VV, cx.VV.n)
    return (cx.VV, cx.VV.n)


def emit_amax_row(cx, strips, ta, tb, Tc, lnw):
    P = cx.P
    P.op("s", lambda e, a0=strips[0]: e.activation(
        out=ta[:, :Tc], in_=a0, func=AF.Abs),
        wait=[lnw, (cx.VV, cx.VV.n)], inc=[(cx.SS, 1)])
    for i in range(1, len(strips)):
        P.op("s", lambda e, a1=strips[i]: e.activation(
            out=tb[:, :Tc], in_=a1, func=AF.Abs),
            wait=[(cx.VV, cx.VV.n)], inc=[(cx.SS, 1)])
        P.op("v", lambda e: e.tensor_tensor(
            out=ta[:, :Tc], in0=ta[:, :Tc], in1=tb[:, :Tc], op=ALU.max),
            wait=[(cx.SS, cx.SS.n)], inc=[(cx.VV, 1)])
    emit_pmax_row(cx, ta, tb, Tc)


def emit_round_to_int(cx, vv, sxb, out_bf, A, Bx, t0, t1, Tc, levels):
    P = cx.P
    P.op("v", lambda e, o=Bx, i0=vv, i1=sxb: e.tensor_mul(out=o, in0=i0, in1=i1),
         inc=[(cx.VV, 1)])
    P.op("s", lambda e, o=A, i_=Bx: e.activation(out=o, in_=i_, func=AF.Abs),
         wait=[(cx.VV, cx.VV.n)], inc=[(cx.SS, 1)])
    P.op("s", lambda e, o=t0, i_=Bx: e.activation(out=o, in_=i_, func=AF.Sign),
         inc=[(cx.SS, 1)])
    P.op("v", lambda e, o=A, i_=A: e.tensor_scalar_add(
        out=o, in0=i_, scalar1=0.5), wait=[(cx.SS, cx.SS.n)])
    emit_floor(cx, A, t1, Bx, levels)
    P.op("v", lambda e, o=t1, i0=t1, i1=A: e.tensor_sub(out=o, in0=i0, in1=i1))
    P.op("v", lambda e, o=out_bf, i0=t1, i1=t0: e.tensor_mul(
        out=o, in0=i0, in1=i1), inc=[(cx.VV, 1)])


def emit_l1f(cx):
    P, nc = cx.P, cx.nc
    Tc = 256
    nch = T // Tc
    wg = cx.win("l1f_wg", [32, 1, 128, KU])
    wu = cx.win("l1f_wu", [32, 1, 128, KU])
    wd = cx.win("l1f_wd", [8, 4, 128, KU])
    qg = nc.dram_tensor("l1f_qg", [32, 1, 128, KU], BF16, kind="Internal").ap()
    qu = nc.dram_tensor("l1f_qu", [32, 1, 128, KU], BF16, kind="Internal").ap()
    qd = nc.dram_tensor("l1f_qd", [8, 4, 128, KU], BF16, kind="Internal").ap()
    last = cx.cur_block_last

    with contextlib.ExitStack() as les:
        wabs = les.enter_context(nc.sbuf_tensor("l1f_wabs", [128, KU], F32))[:, :]
        wsg = les.enter_context(nc.sbuf_tensor("l1f_wsg", [128, KU], F32))[:, :]
        wq = les.enter_context(nc.sbuf_tensor("l1f_wq", [128, KU], BF16))[:, :]
        swb = les.enter_context(nc.sbuf_tensor("l1f_swb", [128, 4], F32))
        rring = les.enter_context(nc.sbuf_tensor("l1f_rr", [128, RING_SLOTS * KU], BF16))
        hbuf = les.enter_context(nc.sbuf_tensor("l1f_h", [128, 32 * 256], F32))
        qx = les.enter_context(nc.sbuf_tensor("l1f_qx", [128, NSTR * 256], BF16))
        qh = les.enter_context(nc.sbuf_tensor("l1f_qh", [128, 32 * 256], BF16))
        ta = les.enter_context(nc.sbuf_tensor("l1f_ta", [128, 1024], F32))
        tb = les.enter_context(nc.sbuf_tensor("l1f_tb", [128, 1024], F32))
        t0 = les.enter_context(nc.sbuf_tensor("l1f_t0", [128, 256], F32))[:, :]
        t1 = les.enter_context(nc.sbuf_tensor("l1f_t1", [128, 256], F32))[:, :]
        sxb = les.enter_context(nc.sbuf_tensor("l1f_sxb", [128, 256], F32))[:, :]
        tg = les.enter_context(nc.sbuf_tensor("l1f_tg", [128, 256], F32))[:, :]
        tu = les.enter_context(nc.sbuf_tensor("l1f_tu", [128, 256], F32))[:, :]
        td = les.enter_context(nc.sbuf_tensor("l1f_td", [128, 256], F32))[:, :]

        st_ = cx.stats
        row = lambda j: st_[0:1, j * TC:j * TC + 256]

        sums = []
        for wi, (wdr, n_units) in enumerate(((wg, 32), (wu, 32), (wd, 32))):
            bsum = psum_acquire(cx)
            bw = bwait(bsum)
            n_m = wdr.shape[0]
            n_ku = wdr.shape[1]
            tot = n_m * n_ku
            idx = 0
            for m in range(n_m):
                for kuu in range(n_ku):
                    slot, wld = stream_unit(cx, wdr[m, kuu])
                    P.op("s", lambda e, o=wabs, i=slot: e.activation(
                        out=o, in_=i, func=AF.Abs),
                        wait=[wld, (cx.TT, cx.TT.n)], inc=[(cx.SS, 1), (cx.w_free, 1)])
                    for hf in range(2):
                        P.op("t", lambda e, o=bsum["h"][0:1, :512],
                             r=wabs[:, hf * 512:(hf + 1) * 512],
                             st=(idx == 0), sp=(idx == 2 * tot - 1):
                             e.matmul(out=o, lhsT=cx.ones_col, rhs=r, start=st, stop=sp),
                             wait=[(cx.SS, cx.SS.n)] + (bw if idx == 0 else []) + cx.const_wait,
                             inc=[(cx.TT, 1)])
                        idx += 1
            K_ = n_ku * KU if wdr is not wd else 4096
            nel = float((4096 * 1024) if wdr is not wd else (1024 * 4096))
            P.op("v", lambda e, o=row(7), i=bsum["h"][0:1, :512]: e.tensor_reduce(
                out=o[:, 0:1], in_=i, axis=AX.X, op=ALU.add),
                wait=[(cx.TT, cx.TT.n)])
            P.op("v", lambda e, o=row(7), i=row(7), s=1.0 / nel: e.tensor_scalar(
                out=o[:, 1:2], in0=i[:, 0:1], scalar1=s, scalar2=1e-5,
                op0=ALU.mult, op1=ALU.max))
            P.op("v", lambda e, o=row(7), i=row(7), w_=wi: e.reciprocal(
                out=o[:, 2 + w_:3 + w_], in_=i[:, 1:2]), inc=[(cx.VV, 1)])
            P.op("v", lambda e, o=st_, i=row(7), w_=wi: e.tensor_copy(
                out=o[0:1, 6 * TC + w_:6 * TC + w_ + 1], in_=i[:, 1:2]),
                inc=[(cx.VV, 1)])
            bsum["guard"] = (cx.VV, cx.VV.n)
            bb = psum_acquire(cx)
            P.op("t", lambda e, o=bb["h"][0:128, 0:1], i=row(7), w_=wi: e.matmul(
                out=o, lhsT=cx.onesr, rhs=i[:, 2 + w_:3 + w_], start=True, stop=True),
                wait=bwait(bb) + [(cx.VV, cx.VV.n)], inc=[(cx.TT, 1)])
            P.op("v", lambda e, o=swb, i=bb["h"], w_=wi: e.tensor_copy(
                out=o[:, w_:w_ + 1], in_=i[0:128, 0:1]),
                wait=[(cx.TT, cx.TT.n)], inc=[(cx.VV, 1)])
            bb["guard"] = (cx.VV, cx.VV.n)
        for wi, (wdr, qdr) in enumerate(((wg, qg), (wu, qu), (wd, qd))):
            n_m, n_ku = wdr.shape[0], wdr.shape[1]
            for m in range(n_m):
                for kuu in range(n_ku):
                    slot, wld = stream_unit(cx, wdr[m, kuu])
                    P.op("s", lambda e, o=wabs, i=slot: e.activation(
                        out=o, in_=i, func=AF.Abs),
                        wait=[wld, (cx.LX, cx.LX.n), (cx.VV, cx.VV.n)],
                        inc=[(cx.SS, 1)])
                    P.op("s", lambda e, o=wsg, i=slot: e.activation(
                        out=o, in_=i, func=AF.Sign), inc=[(cx.SS, 1), (cx.w_free, 1)])
                    P.op("v", lambda e, o=wabs, i=wabs, w_=wi: e.tensor_scalar(
                        out=o, in0=i, scalar1=swb[:, w_:w_ + 1], scalar2=0.5,
                        op0=ALU.mult, op1=ALU.is_gt), wait=[(cx.SS, cx.SS.n)])
                    P.op("v", lambda e, o=wq, i0=wabs, i1=wsg: e.tensor_mul(
                        out=o, in0=i0, in1=i1), inc=[(cx.VV, 1)])
                    prev = cx.LX.n
                    P.op("g", lambda e, s=wq, d=qdr[m, kuu]: e.dma_start(
                        out=d, in_=s),
                        wait=[(cx.VV, cx.VV.n)] + ([(cx.LX, prev)] if prev else []),
                        inc=[(cx.LX, 16)])

        xdst = cx.xdst_for(last)
        for c in range(nch):
            lw = load_x_chunk_g(cx, c, Tc)
            xs = [cx.xbuf[:, i * Tc:(i + 1) * Tc] for i in range(NSTR)]
            xn = [cx.xnbuf[:, i * Tc:(i + 1) * Tc] for i in range(NSTR)]
            P.op("v", wait=[lw])
            P.op("s", wait=[lw])
            P.op("t", wait=[lw])
            lnw = emit_ln(cx, xs, xn, Tc, NSTR)
            emit_amax_row(cx, xn, ta, tb, Tc, lnw)
            P.op("v", lambda e, o=row(0), i=ta[0:1, :256]: e.tensor_single_scalar(
                out=o, in_=i, scalar=1e-5, op=ALU.max))
            P.op("v", lambda e, o=row(1), i=row(0): e.reciprocal(out=o, in_=i))
            P.op("v", lambda e, o=row(1), i=row(1): e.tensor_scalar_mul(
                out=o, in0=i, scalar1=127.0))
            P.op("v", lambda e, o=row(2), i=row(0): e.tensor_scalar_mul(
                out=o, in0=i, scalar1=1.0 / 127.0), inc=[(cx.VV, 1)])
            emit_bcast_tile(cx, row(1), sxb, Tc, extra_wait=[(cx.VV, cx.VV.n)])
            for i in range(NSTR):
                emit_round_to_int(cx, xn[i], sxb, qx[:, i * Tc:(i + 1) * Tc],
                                  ta[:, :Tc], tb[:, :Tc], t0, t1, Tc,
                                  (64, 32, 16, 8, 4, 2, 1))
            qxw = (cx.VV, cx.VV.n)
            P.op("v", lambda e, o=row(3), i=row(2): e.tensor_scalar(
                out=o, in0=i, scalar1=st_[0:1, 6 * TC + 0:6 * TC + 1],
                scalar2=None, op0=ALU.mult))
            P.op("v", lambda e, o=row(4), i=row(2): e.tensor_scalar(
                out=o, in0=i, scalar1=st_[0:1, 6 * TC + 1:6 * TC + 2],
                scalar2=None, op0=ALU.mult), inc=[(cx.VV, 1)])
            emit_bcast_tile(cx, row(3), tg, Tc, extra_wait=[(cx.VV, cx.VV.n)])
            emit_bcast_tile(cx, row(4), tu, Tc)
            for m in range(32):
                bg = psum_acquire(cx)
                bu = psum_acquire(cx)
                for which, (qdr, bk) in enumerate(((qg, bg), (qu, bu))):
                    slot, wld = stream_unit_bf(cx, qdr[m, 0], rring)
                    for k in range(8):
                        P.op("t", lambda e, o=bk["h"][:, :Tc],
                             l=slot[:, k * 128:(k + 1) * 128], r=qx[:, k * Tc:(k + 1) * Tc],
                             st=(k == 0), sp=(k == 7):
                             e.matmul(out=o, lhsT=l, rhs=r, start=st, stop=sp),
                             wait=([wld, qxw] if k == 0 else []) + (bwait(bk) if k == 0 else []),
                             inc=([(cx.w_free, 1), (cx.TT, 1)] if k == 7 else []))
                tmm = (cx.TT, cx.TT.n)
                hst = hbuf[:, m * Tc:(m + 1) * Tc]
                P.op("v", lambda e, o=t0, i0=bg["h"][:, :Tc], i1=tg: e.tensor_mul(
                    out=o, in0=i0, in1=i1), wait=[tmm])
                P.op("v", lambda e, o=t1, i0=bu["h"][:, :Tc], i1=tu: e.tensor_mul(
                    out=o, in0=i0, in1=i1), inc=[(cx.VV, 1)])
                bg["guard"] = (cx.VV, cx.VV.n)
                bu["guard"] = (cx.VV, cx.VV.n)
                P.op("s", lambda e, o=t0, i_=t0: e.activation(
                    out=o, in_=i_, func=AF.Silu), wait=[(cx.VV, cx.VV.n)],
                    inc=[(cx.SS, 1)])
                P.op("v", lambda e, o=hst, i0=t0, i1=t1: e.tensor_mul(
                    out=o, in0=i0, in1=i1), wait=[(cx.SS, cx.SS.n)],
                    inc=[(cx.VV, 1)])
            hw_ = (cx.VV, cx.VV.n)
            hstrips = [hbuf[:, m * Tc:(m + 1) * Tc] for m in range(32)]
            emit_amax_row(cx, hstrips, ta, tb, Tc, hw_)
            P.op("v", lambda e, o=row(0), i=ta[0:1, :256]: e.tensor_single_scalar(
                out=o, in_=i, scalar=1e-5, op=ALU.max))
            P.op("v", lambda e, o=row(1), i=row(0): e.reciprocal(out=o, in_=i))
            P.op("v", lambda e, o=row(1), i=row(1): e.tensor_scalar_mul(
                out=o, in0=i, scalar1=127.0))
            P.op("v", lambda e, o=row(2), i=row(0): e.tensor_scalar_mul(
                out=o, in0=i, scalar1=1.0 / 127.0), inc=[(cx.VV, 1)])
            emit_bcast_tile(cx, row(1), sxb, Tc, extra_wait=[(cx.VV, cx.VV.n)])
            for m in range(32):
                emit_round_to_int(cx, hstrips[m], sxb, qh[:, m * Tc:(m + 1) * Tc],
                                  ta[:, :Tc], tb[:, :Tc], t0, t1, Tc,
                                  (64, 32, 16, 8, 4, 2, 1))
            qhw = (cx.VV, cx.VV.n)
            P.op("v", lambda e, o=row(3), i=row(2): e.tensor_scalar(
                out=o, in0=i, scalar1=st_[0:1, 6 * TC + 2:6 * TC + 3],
                scalar2=None, op0=ALU.mult), inc=[(cx.VV, 1)])
            emit_bcast_tile(cx, row(3), td, Tc, extra_wait=[(cx.VV, cx.VV.n)])
            evw = []
            for m in range(8):
                bk = psum_acquire(cx)
                for kuu in range(4):
                    slot, wld = stream_unit_bf(cx, qd[m, kuu], rring)
                    for kk in range(8):
                        k = kuu * 8 + kk
                        P.op("t", lambda e, o=bk["h"][:, :Tc],
                             l=slot[:, kk * 128:(kk + 1) * 128],
                             r=qh[:, k * Tc:(k + 1) * Tc],
                             st=(k == 0), sp=(k == 31):
                             e.matmul(out=o, lhsT=l, rhs=r, start=st, stop=sp),
                             wait=([wld, qhw] if kk == 0 else []) + (bwait(bk) if k == 0 else []),
                             inc=([(cx.w_free, 1)] if kk == 7 else []) +
                                 ([(cx.TT, 1)] if k == 31 else []))
                tmm = (cx.TT, cx.TT.n)
                P.op("v", lambda e, o=t0, i0=bk["h"][:, :Tc], i1=td: e.tensor_mul(
                    out=o, in0=i0, in1=i1), wait=[tmm], inc=[(cx.VV, 1)])
                bk["guard"] = (cx.VV, cx.VV.n)
                P.op("v", lambda e, o=xs[m], i0=t0, i1=xs[m]: e.tensor_add(
                    out=o, in0=i0, in1=i1), inc=[(cx.VV, 1)])
                evw.append((cx.VV, cx.VV.n))
            store_x_chunk_g(cx, c, cx.xbuf, xdst, Tc, wait=evw[-1:])
    cx.xsrc = xdst


def load_x_chunk_g(cx, c, Tc):
    src = cx.xsrc[:, c * Tc:(c + 1) * Tc].rearrange("(s p) c -> p s c", p=128)
    dv = cx.xbuf[:, 0:NSTR * Tc].rearrange("p (s c) -> p s c", c=Tc)
    prev = cx.LX.n
    cx.P.op("g", lambda e, s=src, d=dv: e.dma_start(out=d, in_=s),
            wait=([(cx.LX, prev)] if prev else []), inc=[(cx.LX, 16)])
    return (cx.LX, cx.LX.n)


def store_x_chunk_g(cx, c, src_buf, dst_dram, Tc, wait):
    dst = dst_dram[:, c * Tc:(c + 1) * Tc].rearrange("(s p) c -> p s c", p=128)
    sv = src_buf[:, 0:NSTR * Tc].rearrange("p (s c) -> p s c", c=Tc)
    prev = cx.LX.n
    cx.P.op("g", lambda e, s=sv, d=dst: e.dma_start(out=d, in_=s),
            wait=list(wait) + ([(cx.LX, prev)] if prev else []),
            inc=[(cx.LX, 16)])
    return (cx.LX, cx.LX.n)


def stream_unit_bf(cx, src_ap, ring_bf):
    sid = cx.slab_id
    cx.slab_id += 1
    j = sid % RING_SLOTS
    slot = ring_bf[:, j * KU:(j + 1) * KU]
    need = sid - (RING_SLOTS - 1)
    cx.slot_uses[j] += 1
    cx.P.op("y", lambda e, s=src_ap, d=slot: e.dma_start(out=d, in_=s),
            wait=([(cx.w_free, need)] if need > 0 else []),
            inc=[(cx.wslot[j], 16)])
    return slot, (cx.wslot[j], 16 * cx.slot_uses[j])





FP4_MID = (0.25, 0.75, 1.25, 1.75, 2.5, 3.5, 5.0)
FP4_DEL = (0.5, 0.5, 0.5, 0.5, 1.0, 1.0, 2.0)


def emit_fp4_quant(cx, src, dst_hi, dst_lo, S0, S1, S2, S3, S4, wait_in=()):
    P = cx.P
    P.op("s", lambda e, o=S0, i=src: e.activation(out=o, in_=i, func=AF.Abs),
         wait=list(wait_in) + [(cx.VV, cx.VV.n)], inc=[(cx.SS, 1)])
    P.op("v", wait=[(cx.SS, cx.SS.n)])
    for sh in (8, 4, 2, 1):
        mask = [(i // 16) * 16 + (i % 16 + sh) % 16 for i in range(32)]
        P.op("v", lambda e, m_=mask, o=S1, i=S0: e.stream_shuffle(
            out=o, in_=i, mask=m_))
        P.op("v", lambda e, o=S0, i0=S0, i1=S1: e.tensor_tensor(
            out=o, in0=i0, in1=i1, op=ALU.max))
    P.op("v", lambda e, o=S0, i=S0: e.tensor_scalar(
        out=o, in0=i, scalar1=1.0 / 6.0, scalar2=1e-12, op0=ALU.mult,
        op1=ALU.max))
    P.op("v", lambda e, o=S1, i=S0: e.reciprocal(out=o, in_=i))
    P.op("v", lambda e, o=S2, i0=src, i1=S1: e.tensor_mul(out=o, in0=i0, in1=i1),
         inc=[(cx.VV, 1)])
    P.op("s", lambda e, o=S1, i=S2: e.activation(out=o, in_=i, func=AF.Abs),
         wait=[(cx.VV, cx.VV.n)], inc=[(cx.SS, 1)])
    P.op("s", lambda e, o=S2, i=S2: e.activation(out=o, in_=i, func=AF.Sign),
         inc=[(cx.SS, 1)])
    P.op("v", lambda e, o=S3, i=S1: e.tensor_scalar(
        out=o, in0=i, scalar1=FP4_MID[0], scalar2=FP4_DEL[0],
        op0=ALU.is_gt, op1=ALU.mult), wait=[(cx.SS, cx.SS.n)])
    for k in range(1, 7):
        P.op("v", lambda e, o=S4, i=S1, m_=FP4_MID[k], d_=FP4_DEL[k]:
             e.tensor_scalar(out=o, in0=i, scalar1=m_, scalar2=d_,
                             op0=ALU.is_gt, op1=ALU.mult))
        P.op("v", lambda e, o=S3, i1=S4: e.tensor_add(out=o, in0=o, in1=i1))
    P.op("v", lambda e, o=S3, i1=S2: e.tensor_mul(out=o, in0=o, in1=i1))
    P.op("v", lambda e, o=S3, i1=S0: e.tensor_mul(out=o, in0=o, in1=i1))
    P.op("v", lambda e, o=dst_hi, i=S3: e.tensor_copy(out=o, in_=i))
    P.op("v", lambda e, o=dst_lo, i0=dst_hi, i1=S3: e.scalar_tensor_tensor(
        out=o, in0=i0, scalar=-1.0, in1=i1, op0=ALU.mult, op1=ALU.add),
        inc=[(cx.VV, 1)])
    return (cx.VV, cx.VV.n)


def emit_mm3(cx, bk, hi_slot, lo_slot, kk, xhi, xlo, Tc, first, last,
             free_slots=False, inc_tt=False):
    P = cx.P
    tiles = ((hi_slot, xhi), (hi_slot, xlo), (lo_slot, xhi))
    for j, (sl, xr) in enumerate(tiles):
        st = first and j == 0
        sp = last and j == 2
        inc = []
        if free_slots and j == 1:
            inc.append((cx.w_free, 1))
        if free_slots and j == 2:
            inc.append((cx.w_free, 1))
        if inc_tt and j == 2:
            inc.append((cx.TT, 1))
        P.op("t", lambda e, o=bk["h"][:, :Tc], l=sl[:, kk * 128:(kk + 1) * 128],
             r=xr, st=st, sp=sp: e.matmul(out=o, lhsT=l, rhs=r, start=st, stop=sp),
             wait=(bwait(bk) if st else []), inc=inc)


def emit_l2f(cx):
    P, nc = cx.P, cx.nc
    Tc = 256
    nch = T // Tc
    NI = 22
    wg = cx.win("l2f_wg", [22, 1, 128, KU])
    wu = cx.win("l2f_wu", [22, 1, 128, KU])
    wd = cx.win("l2f_wd", [8, 3, 128, KU])
    qd = {}
    for nm, shp in (("qg", [22, 1, 128, KU]), ("qu", [22, 1, 128, KU]),
                    ("qdn", [8, 3, 128, KU])):
        qd[nm + "h"] = nc.dram_tensor(f"l2f_{nm}h", shp, BF16, kind="Internal").ap()
        qd[nm + "l"] = nc.dram_tensor(f"l2f_{nm}l", shp, BF16, kind="Internal").ap()
    last = cx.cur_block_last

    with contextlib.ExitStack() as les:
        S = [les.enter_context(nc.sbuf_tensor(f"l2f_s{i}", [128, KU], F32))[:, :]
             for i in range(5)]
        uhi = les.enter_context(nc.sbuf_tensor("l2f_uhi", [128, KU], BF16))[:, :]
        ulo = les.enter_context(nc.sbuf_tensor("l2f_ulo", [128, KU], BF16))[:, :]
        rrh = les.enter_context(nc.sbuf_tensor("l2f_rrh", [128, RING_SLOTS * KU], BF16))
        rrl = les.enter_context(nc.sbuf_tensor("l2f_rrl", [128, RING_SLOTS * KU], BF16))
        hbuf = les.enter_context(nc.sbuf_tensor("l2f_h", [128, NI * 256], F32))
        xqh = les.enter_context(nc.sbuf_tensor("l2f_xqh", [128, NSTR * 256], BF16))
        xql = les.enter_context(nc.sbuf_tensor("l2f_xql", [128, NSTR * 256], BF16))
        hqh = les.enter_context(nc.sbuf_tensor("l2f_hqh", [128, NI * 256], BF16))
        hql = les.enter_context(nc.sbuf_tensor("l2f_hql", [128, NI * 256], BF16))

        for wdr, hih, lol in ((wg, qd["qgh"], qd["qgl"]),
                              (wu, qd["quh"], qd["qul"]),
                              (wd, qd["qdnh"], qd["qdnl"])):
            n_m, n_ku = wdr.shape[0], wdr.shape[1]
            for m in range(n_m):
                for kuu in range(n_ku):
                    slot, wld = stream_unit(cx, wdr[m, kuu])
                    qw = emit_fp4_quant(cx, slot, uhi, ulo, S[0], S[1], S[2],
                                        S[3], S[4],
                                        wait_in=[wld, (cx.LX, cx.LX.n)])
                    P.op("v", lambda e: e.engine_nop(), inc=[(cx.w_free, 1)])
                    prev = cx.LX.n
                    P.op("g", lambda e, s=uhi, d=hih[m, kuu]: e.dma_start(
                        out=d, in_=s), wait=[qw] + ([(cx.LX, prev)] if prev else []),
                        inc=[(cx.LX, 16)])
                    prev = cx.LX.n
                    P.op("g", lambda e, s=ulo, d=lol[m, kuu]: e.dma_start(
                        out=d, in_=s), wait=[(cx.LX, prev)], inc=[(cx.LX, 16)])

        xdst = cx.xdst_for(last)
        for c in range(nch):
            lw = load_x_chunk_g(cx, c, Tc)
            xs = [cx.xbuf[:, i * Tc:(i + 1) * Tc] for i in range(NSTR)]
            xn = [cx.xnbuf[:, i * Tc:(i + 1) * Tc] for i in range(NSTR)]
            P.op("v", wait=[lw])
            P.op("s", wait=[lw])
            P.op("t", wait=[lw])
            lnw = emit_ln(cx, xs, xn, Tc, NSTR)
            for i in range(NSTR):
                emit_fp4_quant(cx, xn[i], xqh[:, i * Tc:(i + 1) * Tc],
                               xql[:, i * Tc:(i + 1) * Tc],
                               S[0][:, :Tc], S[1][:, :Tc], S[2][:, :Tc],
                               S[3][:, :Tc], S[4][:, :Tc],
                               wait_in=([lnw] if i == 0 else []))
            qxw = (cx.VV, cx.VV.n)
            xhi = [xqh[:, i * Tc:(i + 1) * Tc] for i in range(NSTR)]
            xlo = [xql[:, i * Tc:(i + 1) * Tc] for i in range(NSTR)]
            for m in range(NI):
                bg = psum_acquire(cx)
                bu = psum_acquire(cx)
                for which, (hih, lol, bk) in enumerate(
                        ((qd["qgh"], qd["qgl"], bg), (qd["quh"], qd["qul"], bu))):
                    sh, wldh = stream_unit_bf(cx, hih[m, 0], rrh)
                    sl_, wldl = stream_unit_bf(cx, lol[m, 0], rrl)
                    for k in range(8):
                        P.op("t", wait=([wldh, wldl, qxw] if k == 0 else []))
                        emit_mm3(cx, bk, sh, sl_, k, xhi[k], xlo[k], Tc,
                                 first=(k == 0), last=(k == 7),
                                 free_slots=(k == 7), inc_tt=(k == 7))
                tmm = (cx.TT, cx.TT.n)
                hst = hbuf[:, m * Tc:(m + 1) * Tc]
                P.op("s", lambda e, o=S[0][:, :Tc], i=bg["h"][:, :Tc]:
                     e.activation(out=o, in_=i, func=AF.Silu),
                     wait=[tmm, (cx.VV, cx.VV.n)], inc=[(cx.SS, 1)])
                P.op("v", lambda e, o=hst, i0=S[0][:, :Tc], i1=bu["h"][:, :Tc]:
                     e.tensor_mul(out=o, in0=i0, in1=i1),
                     wait=[(cx.SS, cx.SS.n)], inc=[(cx.VV, 1)])
                bg["guard"] = (cx.VV, cx.VV.n)
                bu["guard"] = (cx.VV, cx.VV.n)
            for m in range(NI):
                emit_fp4_quant(cx, hbuf[:, m * Tc:(m + 1) * Tc],
                               hqh[:, m * Tc:(m + 1) * Tc],
                               hql[:, m * Tc:(m + 1) * Tc],
                               S[0][:, :Tc], S[1][:, :Tc], S[2][:, :Tc],
                               S[3][:, :Tc], S[4][:, :Tc])
            qhw = (cx.VV, cx.VV.n)
            hhi = [hqh[:, i * Tc:(i + 1) * Tc] for i in range(NI)]
            hlo = [hql[:, i * Tc:(i + 1) * Tc] for i in range(NI)]
            evw = []
            for m in range(8):
                bk = psum_acquire(cx)
                for kuu in range(3):
                    sh, wldh = stream_unit_bf(cx, qd["qdnh"][m, kuu], rrh)
                    sl_, wldl = stream_unit_bf(cx, qd["qdnl"][m, kuu], rrl)
                    for kk in range(8):
                        k = kuu * 8 + kk
                        if k >= NI:
                            break
                        P.op("t", wait=([wldh, wldl, qhw] if kk == 0 else []))
                        emit_mm3(cx, bk, sh, sl_, kk, hhi[k], hlo[k], Tc,
                                 first=(k == 0), last=(k == NI - 1),
                                 free_slots=(kk == 7 or k == NI - 1),
                                 inc_tt=(k == NI - 1))
                tmm = (cx.TT, cx.TT.n)
                P.op("v", lambda e, o=xs[m], i0=bk["h"][:, :Tc], i1=xs[m]:
                     e.tensor_add(out=o, in0=i0, in1=i1), wait=[tmm],
                     inc=[(cx.VV, 1)])
                bk["guard"] = (cx.VV, cx.VV.n)
                evw.append((cx.VV, cx.VV.n))
            store_x_chunk_g(cx, c, cx.xbuf, xdst, Tc, wait=evw[-1:])
    cx.xsrc = xdst





def emit_l3a(cx):
    P, nc = cx.P, cx.nc
    Tc = 256
    nch = T // Tc
    chunks_per_seq = S // Tc
    NH = 16
    inw = cx.win("l3a_inw", [32, 1, 128, KU])
    inb = cx.win("l3a_inb", [128, 32])
    gb = cx.win("l3a_gb", [128, 16])
    dww = cx.win("l3a_dww", [128, 16 * 7])
    dwb = cx.win("l3a_dwb", [128, 16])
    ldec = cx.win("l3a_ldec", [128, 16])
    mixw = cx.win("l3a_mixw", [16, 2, 128, KU])
    mixb = cx.win("l3a_mixb", [128, 16])
    outw = cx.win("l3a_outw", [8, 2, 128, KU])
    outb = cx.win("l3a_outb", [128, 8])
    last = cx.cur_block_last

    with contextlib.ExitStack() as les:
        sgb = les.enter_context(nc.sbuf_tensor("l3a_sgb", [128, 16], F32))
        sinb = les.enter_context(nc.sbuf_tensor("l3a_sinb", [128, 32], F32))
        sdww = les.enter_context(nc.sbuf_tensor("l3a_sdww", [128, 112], F32))
        sdwb = les.enter_context(nc.sbuf_tensor("l3a_sdwb", [128, 16], F32))
        sa = les.enter_context(nc.sbuf_tensor("l3a_a", [128, 16], F32))
        smixb = les.enter_context(nc.sbuf_tensor("l3a_smixb", [128, 16], F32))
        soutb = les.enter_context(nc.sbuf_tensor("l3a_soutb", [128, 8], F32))
        G = les.enter_context(nc.sbuf_tensor("l3a_G", [128, NH * Tc], F32))
        V = les.enter_context(nc.sbuf_tensor("l3a_V", [128, NH * Tc], F32))
        VP = les.enter_context(nc.sbuf_tensor("l3a_VP", [128, NH * (Tc + 6)], F32))
        SC = les.enter_context(nc.sbuf_tensor("l3a_SC", [128, NH * Tc], F32))
        DA = les.enter_context(nc.sbuf_tensor("l3a_DA", [128, NH * Tc], F32))
        carry = les.enter_context(nc.sbuf_tensor("l3a_cy", [128, 16], F32))
        halob = les.enter_context(nc.sbuf_tensor("l3a_hal", [128, 96], F32))

        for dr, sb in ((gb, sgb), (inb, sinb), (dww, sdww), (dwb, sdwb),
                       (mixb, smixb), (outb, soutb)):
            const_dma(cx, lambda e, d=sb, s=dr: e.dma_start(out=d[:, :], in_=s))
        ldw = const_dma(cx, lambda e: e.dma_start(out=sa[:, :], in_=ldec))
        P.op("s", lambda e: e.activation(out=sa[:, :], in_=sa[:, :],
                                         func=AF.Sigmoid),
             wait=[ldw], inc=[(cx.SS, 1)])
        aw = (cx.SS, cx.SS.n)

        gams = [sgb[:, i:i + 1] for i in range(8)]
        bets = [sgb[:, 8 + i:9 + i] for i in range(8)]
        xdst = cx.xdst_for(last)
        for c in range(nch):
            seq_start = (c % chunks_per_seq == 0)
            lw = load_x_chunk_g(cx, c, Tc)
            xs = [cx.xbuf[:, i * Tc:(i + 1) * Tc] for i in range(NSTR)]
            xn = [cx.xnbuf[:, i * Tc:(i + 1) * Tc] for i in range(NSTR)]
            P.op("v", wait=[lw])
            P.op("s", wait=[lw])
            P.op("t", wait=[lw])
            lnw = emit_ln(cx, xs, xn, Tc, NSTR, gam=gams, bet=bets)
            P.op("t", wait=[lnw])

            def evac_in(m, b, tmm):
                if m < 16:
                    P.op("s", lambda e, o=G[:, m * Tc:(m + 1) * Tc],
                         i=b["h"][:, :Tc], bi=sinb[:, m:m + 1]: e.activation(
                         out=o, in_=i, func=AF.Sigmoid, bias=bi),
                         wait=[tmm, ldw], inc=[(cx.SS, 1)])
                    b["guard"] = (cx.SS, cx.SS.n)
                else:
                    P.op("v", lambda e, o=V[:, (m - 16) * Tc:(m - 15) * Tc],
                         i=b["h"][:, :Tc], bi=sinb[:, m:m + 1]: e.tensor_scalar(
                         out=o, in0=i, scalar1=bi, scalar2=None, op0=ALU.add),
                         wait=[tmm, ldw], inc=[(cx.VV, 1)])
                    b["guard"] = (cx.VV, cx.VV.n)
            emit_linear(cx, inw, 1024, 4096, xn, Tc, evac_in)

            P.op("v", wait=[(cx.SS, cx.SS.n), (cx.VV, cx.VV.n)])
            for s in range(NH):
                hsl = VP[:, s * (Tc + 6):s * (Tc + 6) + 6]
                if seq_start:
                    P.op("v", lambda e, o=hsl: e.memset(o, 0.0))
                else:
                    P.op("v", lambda e, o=hsl, i=halob[:, s * 6:(s + 1) * 6]:
                         e.tensor_copy(out=o, in_=i))
                P.op("v", lambda e, o=VP[:, s * (Tc + 6) + 6:(s + 1) * (Tc + 6)],
                     i=V[:, s * Tc:(s + 1) * Tc]: e.tensor_copy(out=o, in_=i))
            for s in range(NH):
                P.op("v", lambda e, o=halob[:, s * 6:(s + 1) * 6],
                     i=VP[:, s * (Tc + 6) + Tc:(s + 1) * (Tc + 6)]:
                     e.tensor_copy(out=o, in_=i))
            for s in range(NH):
                base = s * (Tc + 6)
                dst = DA[:, s * Tc:(s + 1) * Tc]
                P.op("v", lambda e, o=dst, i0=VP[:, base:base + Tc],
                     w_=sdww[:, s * 7:s * 7 + 1],
                     bb=sdwb[:, s:s + 1].broadcast_to((128, Tc)):
                     e.scalar_tensor_tensor(out=o, in0=i0, scalar=w_, in1=bb,
                                            op0=ALU.mult, op1=ALU.add))
                for k in range(1, 7):
                    P.op("v", lambda e, o=dst, i0=VP[:, base + k:base + k + Tc],
                         w_=sdww[:, s * 7 + k:s * 7 + k + 1], k_=k:
                         e.scalar_tensor_tensor(out=o, in0=i0, scalar=w_, in1=o,
                                                op0=ALU.mult, op1=ALU.add))
            for s in range(NH):
                ini = 0.0 if seq_start else carry[:, s:s + 1]
                P.op("v", lambda e, o=SC[:, s * Tc:(s + 1) * Tc],
                     d0=sa[:, s:s + 1].broadcast_to((128, Tc)),
                     d1=V[:, s * Tc:(s + 1) * Tc], ini=ini:
                     e.tensor_tensor_scan(out=o, data0=d0, data1=d1,
                                          initial=ini, op0=ALU.mult,
                                          op1=ALU.add), wait=[aw])
                P.op("v", lambda e, o=carry[:, s:s + 1],
                     i=SC[:, (s + 1) * Tc - 1:(s + 1) * Tc]: e.tensor_copy(
                     out=o, in_=i))
            P.op("v", lambda e: e.engine_nop(), inc=[(cx.VV, 1)])
            scw = (cx.VV, cx.VV.n)
            P.op("t", wait=[scw])

            scs = [SC[:, s * Tc:(s + 1) * Tc] for s in range(NH)]

            def evac_mix(m, b, tmm):
                P.op("v", lambda e, o=V[:, m * Tc:(m + 1) * Tc], i0=b["h"][:, :Tc],
                     bb=smixb[:, m:m + 1].broadcast_to((128, Tc)):
                     e.tensor_tensor(out=o, in0=i0, in1=bb, op=ALU.add),
                     wait=[tmm])
                P.op("v", lambda e, o=V[:, m * Tc:(m + 1) * Tc],
                     i1=DA[:, m * Tc:(m + 1) * Tc]: e.tensor_add(
                     out=o, in0=o, in1=i1))
                P.op("v", lambda e, o=G[:, m * Tc:(m + 1) * Tc],
                     i1=V[:, m * Tc:(m + 1) * Tc]: e.tensor_mul(
                     out=o, in0=o, in1=i1), inc=[(cx.VV, 1)])
                b["guard"] = (cx.VV, cx.VV.n)
            emit_linear(cx, mixw, 2048, 2048, scs, Tc, evac_mix)
            gw = (cx.VV, cx.VV.n)
            P.op("t", wait=[gw])

            gs = [G[:, s * Tc:(s + 1) * Tc] for s in range(NH)]

            def evac_out(m, b, tmm):
                P.op("v", lambda e, o=xn[m], i0=b["h"][:, :Tc],
                     bc=soutb[:, m:m + 1], i1=xn[m]: e.scalar_tensor_tensor(
                     out=o, in0=i0, scalar=bc, in1=i1, op0=ALU.add, op1=ALU.add),
                     wait=[tmm], inc=[(cx.VV, 1)])
                b["guard"] = (cx.VV, cx.VV.n)
                P.op("v", lambda e, o=xs[m], i0=xn[m], i1=xs[m]: e.tensor_add(
                     out=o, in0=i0, in1=i1), inc=[(cx.VV, 1)])
            emit_linear(cx, outw, 2048, 1024, gs, Tc, evac_out)
            store_x_chunk_g(cx, c, cx.xbuf, xdst, Tc, wait=[(cx.VV, cx.VV.n)])
    cx.xsrc = xdst





def emit_l1a(cx):
    P, nc = cx.P, cx.nc
    Tc = 512
    QT = 512
    wq = cx.win("l1a_wq", [8, 1, 128, KU])
    wk = cx.win("l1a_wk", [8, 1, 128, KU])
    wv = cx.win("l1a_wv", [8, 1, 128, KU])
    wdn = cx.win("l1a_wdn", [2, 1, 128, KU])
    wupk = cx.win("l1a_wupk", [128, 2048])
    wupv = cx.win("l1a_wupv", [128, 2048])
    wo = cx.win("l1a_wo", [8, 16, 64, 128])
    dct = cx.win("l1a_dec", [28, 128, 512])
    qdr = nc.dram_tensor("l1a_q", [D, T], F32, kind="Internal").ap()
    krd = nc.dram_tensor("l1a_krec", [BPC, D, S], F32, kind="Internal").ap()
    vrd = nc.dram_tensor("l1a_vrec", [BPC, S, D], F32, kind="Internal").ap()
    last = cx.cur_block_last

    with contextlib.ExitStack() as les:
        supk = les.enter_context(nc.sbuf_tensor("l1a_supk", [128, 2048], F32))
        supv = les.enter_context(nc.sbuf_tensor("l1a_supv", [128, 2048], F32))
        lat = les.enter_context(nc.sbuf_tensor("l1a_lat", [128, 2 * S], F32))
        kvb = les.enter_context(nc.sbuf_tensor("l1a_kv", [128, NSTR * Tc], F32))
        wob = les.enter_context(nc.sbuf_tensor("l1a_wob", [64, 2 * 2048], F32))
        qtb = les.enter_context(nc.sbuf_tensor("l1a_qt", [128, NSTR * QT], F32))
        krt = les.enter_context(nc.sbuf_tensor("l1a_krt", [128, 2 * 1024], F32))
        vtt = les.enter_context(nc.sbuf_tensor("l1a_vtt", [128, 2 * 1040], F32))
        dtt = les.enter_context(nc.sbuf_tensor("l1a_dtt", [128, 2 * 512], F32))
        ptb = les.enter_context(nc.sbuf_tensor("l1a_pt", [128, 512], F32))
        obuf = les.enter_context(nc.sbuf_tensor("l1a_ob", [128, 16 * 512], F32))
        tmp = les.enter_context(nc.sbuf_tensor("l1a_tmp", [128, 512], F32))

        ldw = const_dma(cx, lambda e: e.dma_start(out=supk[:, :], in_=wupk))
        ldw = const_dma(cx, lambda e: e.dma_start(out=supv[:, :], in_=wupv))

        xdst = cx.xdst_for(last)
        for seq in range(BPC):
            soff = seq * S
            for cc in range(S // Tc):
                c = seq * (S // Tc) + cc
                lw = load_x_chunk_g(cx, c, Tc)
                xs = [cx.xbuf[:, i * Tc:(i + 1) * Tc] for i in range(NSTR)]
                xn = [cx.xnbuf[:, i * Tc:(i + 1) * Tc] for i in range(NSTR)]
                P.op("v", wait=[lw])
                P.op("s", wait=[lw])
                P.op("t", wait=[lw])
                lnw = emit_ln(cx, xs, xn, Tc, NSTR)
                P.op("t", wait=[lnw])

                def evac_q(m, b, tmm):
                    P.op("v", lambda e, o=qtb[:, m * Tc:(m + 1) * Tc],
                         i=b["h"][:, :Tc]: e.tensor_copy(out=o, in_=i),
                         wait=[tmm], inc=[(cx.VV, 1)])
                    b["guard"] = (cx.VV, cx.VV.n)
                emit_linear(cx, wq, 1024, 1024, xn, Tc, evac_q)
                prev = cx.LX.n
                P.op("g", lambda e, s=qtb[:, :].rearrange("p (s c) -> p s c", c=Tc),
                     d=qdr[:, c * Tc:(c + 1) * Tc].rearrange("(s p) c -> p s c", p=128):
                     e.dma_start(out=d, in_=s),
                     wait=[(cx.VV, cx.VV.n)] + ([(cx.LX, prev)] if prev else []),
                     inc=[(cx.LX, 16)])
                def evac_k(m, b, tmm):
                    P.op("v", lambda e, o=kvb[:, m * Tc:(m + 1) * Tc],
                         i=b["h"][:, :Tc]: e.tensor_copy(out=o, in_=i),
                         wait=[tmm], inc=[(cx.VV, 1)])
                    b["guard"] = (cx.VV, cx.VV.n)
                emit_linear(cx, wk, 1024, 1024, xn, Tc, evac_k)

                def evac_v(m, b, tmm):
                    P.op("v", lambda e, o=kvb[:, m * Tc:(m + 1) * Tc],
                         i=b["h"][:, :Tc]: e.tensor_add(out=o, in0=o, in1=i),
                         wait=[tmm], inc=[(cx.VV, 1)])
                    b["guard"] = (cx.VV, cx.VV.n)
                emit_linear(cx, wv, 1024, 1024, xn, Tc, evac_v)
                kvw = (cx.VV, cx.VV.n)
                P.op("t", wait=[kvw])
                kvs = [kvb[:, i * Tc:(i + 1) * Tc] for i in range(NSTR)]

                def evac_lat(m, b, tmm):
                    P.op("v", lambda e, o=lat[:, m * S + cc * Tc:m * S + (cc + 1) * Tc],
                         i=b["h"][:, :Tc]: e.tensor_copy(out=o, in_=i),
                         wait=[tmm], inc=[(cx.VV, 1)])
                    b["guard"] = (cx.VV, cx.VV.n)
                emit_linear(cx, wdn, 1024, 256, kvs, Tc, evac_lat)
            latw = (cx.VV, cx.VV.n)
            P.op("t", wait=[latw])

            for m in range(8):
                for tc4 in range(4):
                    b = psum_acquire(cx)
                    for k2 in range(2):
                        P.op("t", lambda e, o=b["h"][:, :512],
                             l=supk[:, k2 * 1024 + m * 128:k2 * 1024 + (m + 1) * 128],
                             r=lat[:, k2 * S + tc4 * 512:k2 * S + (tc4 + 1) * 512],
                             st=(k2 == 0), sp=(k2 == 1):
                             e.matmul(out=o, lhsT=l, rhs=r, start=st, stop=sp),
                             wait=(bwait(b) if k2 == 0 else []) + [ldw],
                             inc=([(cx.TT, 1)] if k2 == 1 else []))
                    tmm = (cx.TT, cx.TT.n)
                    P.op("v", lambda e, o=tmp[:, :], i=b["h"][:, :512]:
                         e.tensor_copy(out=o, in_=i),
                         wait=[tmm, (cx.LX, cx.LX.n)], inc=[(cx.VV, 1)])
                    b["guard"] = (cx.VV, cx.VV.n)
                    prev = cx.LX.n
                    P.op("g", lambda e, s=tmp[:, :],
                         d=krd[seq, m * 128:(m + 1) * 128, tc4 * 512:(tc4 + 1) * 512]:
                         e.dma_start(out=d, in_=s),
                         wait=[(cx.VV, cx.VV.n)] + ([(cx.LX, prev)] if prev else []),
                         inc=[(cx.LX, 16)])
            for tt in range(16):
                for fh in range(2):
                    b = psum_acquire(cx)
                    for k2 in range(2):
                        P.op("t", lambda e, o=b["h"][:, :512],
                             l=lat[:, k2 * S + tt * 128:k2 * S + (tt + 1) * 128],
                             r=supv[:, k2 * 1024 + fh * 512:k2 * 1024 + (fh + 1) * 512],
                             st=(k2 == 0), sp=(k2 == 1):
                             e.matmul(out=o, lhsT=l, rhs=r, start=st, stop=sp),
                             wait=(bwait(b) if k2 == 0 else []),
                             inc=([(cx.TT, 1)] if k2 == 1 else []))
                    tmm = (cx.TT, cx.TT.n)
                    P.op("v", lambda e, o=tmp[:, :], i=b["h"][:, :512]:
                         e.tensor_copy(out=o, in_=i),
                         wait=[tmm, (cx.LX, cx.LX.n)], inc=[(cx.VV, 1)])
                    b["guard"] = (cx.VV, cx.VV.n)
                    prev = cx.LX.n
                    P.op("g", lambda e, s=tmp[:, :],
                         d=vrd[seq, tt * 128:(tt + 1) * 128, fh * 512:(fh + 1) * 512]:
                         e.dma_start(out=d, in_=s),
                         wait=[(cx.VV, cx.VV.n)] + ([(cx.LX, prev)] if prev else []),
                         inc=[(cx.LX, 16)])

            for qt in range(S // QT):
                c = seq * (S // Tc) + qt
                lw = load_x_chunk_g(cx, c, Tc)
                prev = cx.LX.n
                P.op("g", lambda e, s=qdr[:, (soff + qt * QT):(soff + qt * QT) + QT
                     ].rearrange("(s p) c -> p s c", p=128),
                     d=cx.xnbuf[:, 0:NSTR * QT].rearrange("p (s c) -> p s c", c=QT):
                     e.dma_start(out=d, in_=s), wait=[(cx.LX, prev)],
                     inc=[(cx.LX, 16)])
                qw_ = (cx.LX, cx.LX.n)
                P.op("v", lambda e: e.memset(obuf[:, :], 0.0),
                     wait=[(cx.TT, cx.TT.n)], inc=[(cx.VV, 1)])
                for kt in range(16):
                    par = kt % 2
                    n_ = 4 * qt - kt + 15
                    prev = cx.LX.n
                    P.op("g", lambda e, s=krd[seq, :, kt * 128:(kt + 1) * 128
                         ].rearrange("(a p) c -> p a c", p=128),
                         d=krt[:, par * 1024:(par + 1) * 1024
                         ].rearrange("p (a c) -> p a c", c=128):
                         e.dma_start(out=d, in_=s),
                         wait=[(cx.LX, prev)] + list(cx.ktguard[par] or ()),
                         inc=[(cx.LX, 16)])
                    prev = cx.LX.n
                    P.op("g", lambda e,
                         s=vrd[seq, kt * 128:(kt + 1) * 128, :].rearrange(
                             "p (h c) -> p h c", c=64),
                         d=vtt[:, par * 1040:(par + 1) * 1040].rearrange(
                             "p (h c) -> p h c", c=65)[:, :, 0:64]:
                         e.dma_start(out=d, in_=s), wait=[(cx.LX, prev)],
                         inc=[(cx.LX, 16)])
                    prev = cx.LX.n
                    P.op("g", lambda e, s=dct[n_],
                         d=dtt[:, par * 512:(par + 1) * 512]:
                         e.dma_start(out=d, in_=s), wait=[(cx.LX, prev)],
                         inc=[(cx.LX, 16)])
                    ldk = (cx.LX, cx.LX.n)
                    P.op("v", lambda e, o=vtt[:, par * 1040:(par + 1) * 1040
                         ].rearrange("p (h c) -> p h c", c=65)[:, :, 64:65]:
                         e.memset(o, 1.0), wait=[ldk], inc=[(cx.VV, 1)])
                    onesw = (cx.VV, cx.VV.n)
                    for h in range(16):
                        bs_ = psum_acquire(cx)
                        a_ = (h * 64) // 128
                        r0 = (h * 64) % 128
                        lhs = krt[r0:r0 + 64,
                                  par * 1024 + a_ * 128:par * 1024 + (a_ + 1) * 128]
                        qrhs = cx.xnbuf[r0:r0 + 64, a_ * QT:(a_ + 1) * QT]
                        P.op("t", lambda e, o=bs_["h"][:, :QT], l=lhs, r=qrhs:
                             e.matmul(out=o, lhsT=l, rhs=r, start=True, stop=True),
                             wait=bwait(bs_) + [ldk, qw_], inc=[(cx.TT, 1)])
                        tmm = (cx.TT, cx.TT.n)
                        P.op("v", lambda e, o=ptb[:, :], i0=bs_["h"][:, :QT],
                             i1=dtt[:, par * 512:(par + 1) * 512]: e.tensor_mul(
                             out=o, in0=i0, in1=i1), wait=[tmm], inc=[(cx.VV, 1)])
                        bs_["guard"] = (cx.VV, cx.VV.n)
                        P.op("s", lambda e, o=ptb[:, :], i=ptb[:, :]: e.activation(
                             out=o, in_=i, func=AF.Exp), wait=[(cx.VV, cx.VV.n)],
                             inc=[(cx.SS, 1)])
                        bo = psum_acquire(cx)
                        P.op("t", lambda e, o=bo["h"][0:65, :QT],
                             l=vtt[:, par * 1040 + h * 65:par * 1040 + (h + 1) * 65],
                             r=ptb[:, :]: e.matmul(out=o, lhsT=l, rhs=r,
                                                   start=True, stop=True),
                             wait=bwait(bo) + [(cx.SS, cx.SS.n), onesw],
                             inc=[(cx.TT, 1)])
                        tmm2 = (cx.TT, cx.TT.n)
                        P.op("v", lambda e, o=obuf[0:65, h * 512:h * 512 + QT],
                             i=bo["h"][0:65, :QT]: e.tensor_add(out=o, in0=o, in1=i),
                             wait=[tmm2], inc=[(cx.VV, 1)])
                        bo["guard"] = (cx.VV, cx.VV.n)
                    cx.ktguard[par] = ((cx.TT, cx.TT.n), (cx.VV, cx.VV.n))
                for h in range(16):
                    P.op("v", lambda e, o=obuf[64:65, h * 512:h * 512 + QT]:
                         e.reciprocal(out=o, in_=o), inc=[(cx.VV, 1)])
                    bb = psum_acquire(cx)
                    P.op("t", lambda e, o=bb["h"][:, :QT],
                         r=obuf[64:65, h * 512:h * 512 + QT]: e.matmul(
                         out=o, lhsT=cx.onesr64, rhs=r, start=True, stop=True),
                         wait=bwait(bb) + [(cx.VV, cx.VV.n)] + cx.const_wait,
                         inc=[(cx.TT, 1)])
                    tmm = (cx.TT, cx.TT.n)
                    P.op("v", lambda e, o=obuf[0:64, h * 512:h * 512 + QT],
                         i=bb["h"][0:64, :QT]: e.tensor_mul(out=o, in0=o, in1=i),
                         wait=[tmm], inc=[(cx.VV, 1)])
                    bb["guard"] = (cx.VV, cx.VV.n)
                ow = (cx.VV, cx.VV.n)
                P.op("t", wait=[ow])
                for m in range(8):
                    slot, wld = stream_wo(cx, wo[m], wob)
                    bk = psum_acquire(cx)
                    for kb in range(16):
                        P.op("t", lambda e, o=bk["h"][:, :QT],
                             l=slot[0:64, kb * 128:(kb + 1) * 128],
                             r=obuf[0:64, kb * 512:kb * 512 + QT],
                             st=(kb == 0), sp=(kb == 15): e.matmul(
                             out=o, lhsT=l, rhs=r, start=st, stop=sp),
                             wait=([wld] if kb == 0 else []) + (bwait(bk) if kb == 0 else []),
                             inc=([(cx.TT, 1), (cx.wofree, 1)] if kb == 15 else []))
                    tmm = (cx.TT, cx.TT.n)
                    P.op("v", lambda e, o=cx.xbuf[:, m * QT:(m + 1) * QT],
                         i0=bk["h"][:, :QT], i1=cx.xbuf[:, m * QT:(m + 1) * QT]:
                         e.tensor_add(out=o, in0=i0, in1=i1), wait=[tmm, lw],
                         inc=[(cx.VV, 1)])
                    bk["guard"] = (cx.VV, cx.VV.n)
                store_x_chunk_g(cx, c, cx.xbuf, xdst, Tc, wait=[(cx.VV, cx.VV.n)])
    cx.xsrc = xdst


def stream_wo(cx, src_ap, wob):
    sid = cx.wo_id
    cx.wo_id += 1
    j = sid % 2
    slot = wob[0:64, j * 2048:(j + 1) * 2048]
    need = sid - 1
    cx.wo_uses[j] += 1
    cx.P.op("y", lambda e, s=src_ap.rearrange("kb p c -> p kb c"),
            d=slot.rearrange("p (kb c) -> p kb c", c=128):
            e.dma_start(out=d, in_=s),
            wait=([(cx.wofree, need)] if need > 0 else []),
            inc=[(cx.woslot[j], 16)])
    return slot, (cx.woslot[j], 16 * cx.wo_uses[j])





def emit_l2a(cx):
    P, nc = cx.P, cx.nc
    Tc = 512
    CK = 128
    wq = cx.win("l2a_wq", [8, 1, 128, KU])
    wk = cx.win("l2a_wk", [2, 1, 128, KU])
    wv = cx.win("l2a_wv", [2, 1, 128, KU])
    wo = cx.win("l2a_wo", [8, 16, 64, 128])
    ropec = cx.win("l2a_rope", [128, S])
    ropes = cx.win("l2a_ropes", [128, S])
    perm = cx.win("l2a_perm", [128, 128])
    maskt = cx.win("l2a_mask", [128, 16 * 128])
    qdect = cx.win("l2a_qdec", [64, 16 * 128])
    kdect = cx.win("l2a_kdec", [128, 16])
    last = cx.cur_block_last
    slopes = [2.0 ** (-8.0 / 16 * (i + 1)) for i in range(16)]
    blkdec = [float(np.float32(np.exp(np.float64(-sl * 128)))) for sl in slopes]

    with contextlib.ExitStack() as les:
        sco = les.enter_context(nc.sbuf_tensor("l2a_cos", [128, S], F32))
        ssi = les.enter_context(nc.sbuf_tensor("l2a_sin", [128, S], F32))
        spm = les.enter_context(nc.sbuf_tensor("l2a_spm", [128, 128], F32))
        smk = les.enter_context(nc.sbuf_tensor("l2a_mk", [128, 16 * 128], F32))
        sqd = les.enter_context(nc.sbuf_tensor("l2a_qd", [64, 16 * 128], F32))
        skd = les.enter_context(nc.sbuf_tensor("l2a_kd", [128, 16], F32))
        qrp = les.enter_context(nc.sbuf_tensor("l2a_qr", [128, Tc], F32))
        ktm = les.enter_context(nc.sbuf_tensor("l2a_ktm", [128, 4 * 64], F32))
        vtm = les.enter_context(nc.sbuf_tensor("l2a_vtm", [128, 4 * 64], F32))
        qhm = les.enter_context(nc.sbuf_tensor("l2a_qhm", [64, 16 * Tc], F32))
        khm = les.enter_context(nc.sbuf_tensor("l2a_khm", [64, 4 * Tc], F32))
        vhm = les.enter_context(nc.sbuf_tensor("l2a_vhm", [64, 4 * Tc], F32))
        kht = les.enter_context(nc.sbuf_tensor("l2a_kht", [128, 64], F32))
        qht = les.enter_context(nc.sbuf_tensor("l2a_qht", [64, 128], F32))
        smt = les.enter_context(nc.sbuf_tensor("l2a_smt", [128, 128], F32))
        kvs = les.enter_context(nc.sbuf_tensor("l2a_kvs", [64, 16 * 64], F32))
        ob2 = les.enter_context(nc.sbuf_tensor("l2a_ob", [64, 16 * Tc], F32))
        wob = les.enter_context(nc.sbuf_tensor("l2a_wob", [64, 2 * 2048], F32))
        tmp = les.enter_context(nc.sbuf_tensor("l2a_tq", [128, Tc], F32))

        for dr, sb in ((ropec, sco), (ropes, ssi), (perm, spm), (maskt, smk),
                       (qdect, sqd), (kdect, skd)):
            ldw = const_dma(cx, lambda e, d=sb, s=dr: e.dma_start(out=d[:, :], in_=s))

        xdst = cx.xdst_for(last)
        for seq in range(BPC):
            P.op("v", lambda e: e.memset(kvs[:, :], 0.0),
                 wait=[(cx.TT, cx.TT.n)], inc=[(cx.VV, 1)])
            for cc in range(S // Tc):
                c = seq * (S // Tc) + cc
                lw = load_x_chunk_g(cx, c, Tc)
                xs = [cx.xbuf[:, i * Tc:(i + 1) * Tc] for i in range(NSTR)]
                xn = [cx.xnbuf[:, i * Tc:(i + 1) * Tc] for i in range(NSTR)]
                P.op("v", wait=[lw])
                P.op("s", wait=[lw])
                P.op("t", wait=[lw])
                lnw = emit_ln(cx, xs, xn, Tc, NSTR)
                P.op("t", wait=[lnw])
                cs = sco[:, cc * Tc:(cc + 1) * Tc]
                sn = ssi[:, cc * Tc:(cc + 1) * Tc]

                def mk_evac(dsthm, do_rope):
                    def evac(m, b, tmm):
                        if do_rope:
                            P.op("v", lambda e, o=tmp[:, :], i=b["h"][:, :Tc]:
                                 e.tensor_copy(out=o, in_=i),
                                 wait=[tmm, (cx.VV, cx.VV.n), (cx.LX, cx.LX.n)],
                                 inc=[(cx.VV, 1)])
                            b["guard"] = (cx.VV, cx.VV.n)
                            b2 = psum_acquire(cx)
                            P.op("t", lambda e, o=b2["h"][:, :Tc], r=tmp[:, :]:
                                 e.matmul(out=o, lhsT=spm[:, :], rhs=r,
                                          start=True, stop=True),
                                 wait=bwait(b2) + [ldw, (cx.VV, cx.VV.n)],
                                 inc=[(cx.TT, 1)])
                            t2 = (cx.TT, cx.TT.n)
                            P.op("v", lambda e, o=qrp[:, :], i0=b2["h"][:, :Tc],
                                 i1=sn: e.tensor_mul(out=o, in0=i0, in1=i1),
                                 wait=[t2], inc=[(cx.VV, 1)])
                            b2["guard"] = (cx.VV, cx.VV.n)
                            P.op("v", lambda e, o=tmp[:, :], i1=cs: e.tensor_mul(
                                 out=o, in0=o, in1=i1))
                            P.op("v", lambda e, o=qrp[:, :], i1=tmp[:, :]:
                                 e.tensor_add(out=o, in0=o, in1=i1),
                                 inc=[(cx.VV, 1)])
                        else:
                            P.op("v", lambda e, o=qrp[:, :], i=b["h"][:, :Tc]:
                                 e.tensor_copy(out=o, in_=i),
                                 wait=[tmm, (cx.LX, cx.LX.n)], inc=[(cx.VV, 1)])
                            b["guard"] = (cx.VV, cx.VV.n)
                        for half in range(2):
                            prev = cx.LX.n
                            P.op("g", lambda e, s=qrp[half * 64:half * 64 + 64, :],
                                 d=dsthm[:, (2 * m + half) * Tc:(2 * m + half + 1) * Tc]:
                                 e.dma_start(out=d, in_=s),
                                 wait=[(cx.VV, cx.VV.n)] + ([(cx.LX, prev)] if prev else []),
                                 inc=[(cx.LX, 16)])
                    return evac
                emit_linear(cx, wq, 1024, 1024, xn, Tc, mk_evac(qhm, True))
                emit_linear(cx, wk, 1024, 256, xn, Tc, mk_evac(khm, True))
                emit_linear(cx, wv, 1024, 256, xn, Tc, mk_evac(vhm, False))
                hmw = (cx.LX, cx.LX.n)
                P.op("t", wait=[hmw])
                P.op("v", wait=[hmw])

                for ck in range(Tc // CK):
                    col = ck * CK
                    for g in range(4):
                        for srcb, dstb in ((khm, ktm), (vhm, vtm)):
                            bt = psum_acquire(cx)
                            P.op("t", lambda e, o=bt["h"][0:CK, 0:64],
                                 i=srcb[0:64, g * Tc + col:g * Tc + col + CK],
                                 idn=cx.ident2[0:64, 0:64]:
                                 e.transpose(out=o, in_=i, identity=idn),
                                 wait=bwait(bt), inc=[(cx.TT, 1)])
                            tmm = (cx.TT, cx.TT.n)
                            P.op("v", lambda e, o=dstb[:, g * 64:(g + 1) * 64],
                                 i=bt["h"][0:CK, 0:64]: e.tensor_copy(out=o, in_=i),
                                 wait=[tmm], inc=[(cx.VV, 1)])
                            bt["guard"] = (cx.VV, cx.VV.n)
                    tw = (cx.VV, cx.VV.n)
                    P.op("t", wait=[tw])
                    for h in range(16):
                        g = h // 4
                        qfm = qhm[0:64, h * Tc + col:h * Tc + col + CK]
                        kfm = khm[0:64, g * Tc + col:g * Tc + col + CK]
                        bs_ = psum_acquire(cx)
                        P.op("t", lambda e, o=bs_["h"][0:CK, 0:CK], l=kfm, r=qfm:
                             e.matmul(out=o, lhsT=l, rhs=r, start=True, stop=True),
                             wait=bwait(bs_), inc=[(cx.TT, 1)])
                        tmm = (cx.TT, cx.TT.n)
                        P.op("v", lambda e, o=smt[:, :], i0=bs_["h"][0:CK, 0:CK],
                             i1=smk[:, h * 128:(h + 1) * 128]: e.tensor_mul(
                             out=o, in0=i0, in1=i1), wait=[tmm, ldw],
                             inc=[(cx.VV, 1)])
                        bs_["guard"] = (cx.VV, cx.VV.n)
                        P.op("v", lambda e, o=qht[:, :], i0=qfm,
                             i1=sqd[:, h * 128:(h + 1) * 128]: e.tensor_mul(
                             out=o, in0=i0, in1=i1), wait=[(cx.TT, cx.TT.n)])
                        P.op("v", lambda e, o=kht[:, :],
                             i0=vtm if False else ktm[:, g * 64:(g + 1) * 64],
                             sc=skd[:, h:h + 1]: e.tensor_scalar(
                             out=o[:, 0:64], in0=i0, scalar1=sc, scalar2=None,
                             op0=ALU.mult), inc=[(cx.VV, 1)])
                        smw = (cx.VV, cx.VV.n)
                        bo = psum_acquire(cx)
                        P.op("t", lambda e, o=bo["h"][0:64, 0:CK],
                             l=vtm[:, g * 64:(g + 1) * 64], r=smt[:, :]:
                             e.matmul(out=o, lhsT=l, rhs=r, start=True, stop=False),
                             wait=bwait(bo) + [smw])
                        P.op("t", lambda e, o=bo["h"][0:64, 0:CK],
                             l=kvs[:, h * 64:(h + 1) * 64], r=qht[:, :]:
                             e.matmul(out=o, lhsT=l, rhs=r, start=False, stop=True),
                             inc=[(cx.TT, 1)])
                        tmo = (cx.TT, cx.TT.n)
                        P.op("v", lambda e, o=ob2[:, h * Tc + col:h * Tc + col + CK],
                             i=bo["h"][0:64, 0:CK]: e.tensor_copy(out=o, in_=i),
                             wait=[tmo], inc=[(cx.VV, 1)])
                        bo["guard"] = (cx.VV, cx.VV.n)
                        bk = psum_acquire(cx)
                        P.op("t", lambda e, o=bk["h"][0:64, 0:64], l=kht[:, 0:64],
                             r=vtm[:, g * 64:(g + 1) * 64]: e.matmul(
                             out=o, lhsT=l, rhs=r, start=True, stop=True),
                             wait=bwait(bk), inc=[(cx.TT, 1)])
                        tmk = (cx.TT, cx.TT.n)
                        P.op("v", lambda e, o=kvs[:, h * 64:(h + 1) * 64],
                             i0=kvs[:, h * 64:(h + 1) * 64], bd=blkdec[h],
                             i1=bk["h"][0:64, 0:64]: e.scalar_tensor_tensor(
                             out=o, in0=i0, scalar=bd, in1=i1, op0=ALU.mult,
                             op1=ALU.add), wait=[tmk], inc=[(cx.VV, 1)])
                        bk["guard"] = (cx.VV, cx.VV.n)
                obw = (cx.VV, cx.VV.n)
                P.op("t", wait=[obw])
                for m in range(8):
                    slot, wld = stream_wo(cx, wo[m], wob)
                    bkk = psum_acquire(cx)
                    for kb in range(16):
                        P.op("t", lambda e, o=bkk["h"][:, :Tc],
                             l=slot[0:64, kb * 128:(kb + 1) * 128],
                             r=ob2[0:64, kb * Tc:(kb + 1) * Tc],
                             st=(kb == 0), sp=(kb == 15): e.matmul(
                             out=o, lhsT=l, rhs=r, start=st, stop=sp),
                             wait=([wld] if kb == 0 else []) + (bwait(bkk) if kb == 0 else []),
                             inc=([(cx.TT, 1), (cx.wofree, 1)] if kb == 15 else []))
                    tmm = (cx.TT, cx.TT.n)
                    P.op("v", lambda e, o=xs[m], i0=bkk["h"][:, :Tc], i1=xs[m]:
                         e.tensor_add(out=o, in0=i0, in1=i1), wait=[tmm],
                         inc=[(cx.VV, 1)])
                    bkk["guard"] = (cx.VV, cx.VV.n)
                store_x_chunk_g(cx, c, cx.xbuf, xdst, Tc, wait=[(cx.VV, cx.VV.n)])
    cx.xsrc = xdst





def emit_l0a(cx):
    P, nc = cx.P, cx.nc
    Tc = 256
    nch = T // Tc
    cps = S // Tc
    inw = cx.win("l0a_inw", [35, 1, 128, KU])
    lnwb = cx.win("l0a_lnb", [128, 16])
    cvw = cx.win("l0a_convw", [128, 72])
    cvb = cx.win("l0a_convb", [128, 18])
    dtbw = cx.win("l0a_dtb", [128, 1])
    alog = cx.win("l0a_alog", [128, 1])
    dskr = cx.win("l0a_dsk", [1, 32])
    rmsw = cx.win("l0a_rmsw", [128, 16])
    outw = cx.win("l0a_outw", [8, 2, 128, KU])
    w1 = cx.win("l0a_w1", [32, 1, 128, KU])
    b1w = cx.win("l0a_b1", [128, 32])
    w2 = cx.win("l0a_w2", [8, 4, 128, KU])
    b2w = cx.win("l0a_b2", [128, 8])
    utri = cx.win("l0a_utri", [128, 128])
    sel127 = cx.win("l0a_sel", [128, 1])
    rsel = cx.win("l0a_rsel", [32, 32 * 128])
    last = cx.cur_block_last

    with contextlib.ExitStack() as les:
        slnw = les.enter_context(nc.sbuf_tensor("l0a_slnb", [128, 16], F32))
        scvw = les.enter_context(nc.sbuf_tensor("l0a_scvw", [128, 72], F32))
        scvb = les.enter_context(nc.sbuf_tensor("l0a_scvb", [128, 18], F32))
        sdtb = les.enter_context(nc.sbuf_tensor("l0a_sdtb", [128, 1], F32))
        sacl = les.enter_context(nc.sbuf_tensor("l0a_sacl", [128, 1], F32))
        sdsk = les.enter_context(nc.sbuf_tensor("l0a_sdsk", [1, 32], F32))
        srmw = les.enter_context(nc.sbuf_tensor("l0a_srmw", [128, 16], F32))
        sb1 = les.enter_context(nc.sbuf_tensor("l0a_sb1", [128, 32], F32))
        sb2 = les.enter_context(nc.sbuf_tensor("l0a_sb2", [128, 8], F32))
        sutri = les.enter_context(nc.sbuf_tensor("l0a_sutri", [128, 128], F32))
        ssel = les.enter_context(nc.sbuf_tensor("l0a_ssel", [128, 1], F32))
        srsel = les.enter_context(nc.sbuf_tensor("l0a_srsel", [32, 32 * 128], F32))
        cumbn = les.enter_context(nc.sbuf_tensor("l0a_cumn", [128, 128], F32))
        dcolb = les.enter_context(nc.sbuf_tensor("l0a_dcol", [128, 32], F32))
        zbuf = les.enter_context(nc.sbuf_tensor("l0a_z", [128, 16 * Tc], F32))
        xbcp = les.enter_context(nc.sbuf_tensor("l0a_xbcp", [128, 18 * (Tc + 3)], F32))
        xbco = les.enter_context(nc.sbuf_tensor("l0a_xbco", [128, 18 * Tc], F32))
        halob = les.enter_context(nc.sbuf_tensor("l0a_halo", [128, 54], F32))
        dtbuf = les.enter_context(nc.sbuf_tensor("l0a_dt", [128, Tc], F32))
        adt = les.enter_context(nc.sbuf_tensor("l0a_adt", [128, Tc], F32))
        cumb = les.enter_context(nc.sbuf_tensor("l0a_cum", [128, 128], F32))
        dt_t = les.enter_context(nc.sbuf_tensor("l0a_dtt", [128, 32], F32))
        cum_t = les.enter_context(nc.sbuf_tensor("l0a_cumt", [128, 32], F32))
        ecumb = les.enter_context(nc.sbuf_tensor("l0a_ecum", [128, 32], F32))
        fmat = les.enter_context(nc.sbuf_tensor("l0a_fmat", [128, 32], F32))
        ecole = les.enter_context(nc.sbuf_tensor("l0a_ecole", [128, 32], F32))
        xt = les.enter_context(nc.sbuf_tensor("l0a_xt", [128, 16 * 128], F32))
        btsb = les.enter_context(nc.sbuf_tensor("l0a_bt", [128, 128], F32))
        stsb = les.enter_context(nc.sbuf_tensor("l0a_st", [128, 128], F32))
        mtsb = les.enter_context(nc.sbuf_tensor("l0a_mt", [128, 128], F32))
        bfsb = les.enter_context(nc.sbuf_tensor("l0a_bf", [128, 128], F32))
        ybuf = les.enter_context(nc.sbuf_tensor("l0a_y", [128, 2048], F32))
        yfm = les.enter_context(nc.sbuf_tensor("l0a_yfm", [128, 16 * Tc], F32))
        Hbuf = les.enter_context(nc.sbuf_tensor("l0a_H", [128, 32 * 64], F32))
        midb = les.enter_context(nc.sbuf_tensor("l0a_mid", [128, 8 * Tc], F32))
        abuf = les.enter_context(nc.sbuf_tensor("l0a_ab", [128, 8 * Tc], F32))
        a2b = abuf
        hmlp = les.enter_context(nc.sbuf_tensor("l0a_hm", [128, 32 * 64], F32))

        for dr, sb in ((lnwb, slnw), (cvw, scvw), (cvb, scvb), (dtbw, sdtb),
                       (alog, sacl), (dskr, sdsk), (rmsw, srmw), (b1w, sb1),
                       (b2w, sb2), (utri, sutri), (sel127, ssel),
                       (rsel, srsel)):
            ldw = const_dma(cx, lambda e, d=sb, s=dr: e.dma_start(out=d[:, :], in_=s))
        P.op("s", lambda e: e.activation(out=sacl[:, :], in_=sacl[:, :], func=AF.Exp),
             wait=[ldw], inc=[(cx.SS, 1)])
        P.op("v", lambda e: e.tensor_scalar_mul(out=sacl[:, :], in0=sacl[:, :],
                                                scalar1=-1.0),
             wait=[(cx.SS, cx.SS.n)], inc=[(cx.VV, 1)])
        aw_ = (cx.VV, cx.VV.n)
        bD = psum_acquire(cx)
        sdsk_sb = les.enter_context(nc.sbuf_tensor("l0a_dskro", [1, 32], F32))
        P.op("v", lambda e: e.tensor_copy(out=sdsk_sb[:, :], in_=sdsk[:, :]),
             wait=[ldw], inc=[(cx.VV, 1)])
        P.op("t", lambda e, o=bD["h"][:, 0:32]: e.matmul(
            out=o, lhsT=cx.onesr, rhs=sdsk_sb[:, :], start=True, stop=True),
            wait=bwait(bD) + [(cx.VV, cx.VV.n)] + cx.const_wait, inc=[(cx.TT, 1)])
        P.op("v", lambda e, i=bD["h"][:, 0:32]: e.tensor_copy(out=dcolb[:, :], in_=i),
             wait=[(cx.TT, cx.TT.n)], inc=[(cx.VV, 1)])
        bD["guard"] = (cx.VV, cx.VV.n)

        gams = [slnw[:, i:i + 1] for i in range(8)]
        bets = [slnw[:, 8 + i:9 + i] for i in range(8)]
        xdst = cx.xdst_for(last)
        for c in range(nch):
            seq_start = (c % cps == 0)
            lw = load_x_chunk_g(cx, c, Tc)
            xs = [cx.xbuf[:, i * Tc:(i + 1) * Tc] for i in range(NSTR)]
            xn = [cx.xnbuf[:, i * Tc:(i + 1) * Tc] for i in range(NSTR)]
            P.op("v", wait=[lw])
            P.op("s", wait=[lw])
            P.op("t", wait=[lw])
            lnw1 = emit_ln(cx, xs, xn, Tc, NSTR)
            P.op("t", wait=[lnw1])
            P.op("s", wait=[lnw1])
            ab = [abuf[:, i * Tc:(i + 1) * Tc] for i in range(NSTR)]
            lnw2 = emit_ln(cx, xn, ab, Tc, NSTR, gam=gams, bet=bets)
            P.op("t", wait=[lnw2])

            def evac_in(m, b, tmm):
                if m < 16:
                    P.op("v", lambda e, o=zbuf[:, m * Tc:(m + 1) * Tc],
                         i=b["h"][:, :Tc]: e.tensor_copy(out=o, in_=i),
                         wait=[tmm], inc=[(cx.VV, 1)])
                elif m < 34:
                    s_ = m - 16
                    P.op("v", lambda e, o=xbcp[:, s_ * (Tc + 3) + 3:(s_ + 1) * (Tc + 3)],
                         i=b["h"][:, :Tc]: e.tensor_copy(out=o, in_=i),
                         wait=[tmm], inc=[(cx.VV, 1)])
                else:
                    P.op("v", lambda e, o=dtbuf[0:32, 0:Tc], i=b["h"][0:32, :Tc]:
                         e.tensor_copy(out=o, in_=i), wait=[tmm], inc=[(cx.VV, 1)])
                b["guard"] = (cx.VV, cx.VV.n)
            emit_linear(cx, inw, 1024, 4480, ab, Tc, evac_in)
            inw_ = (cx.VV, cx.VV.n)

            for s_ in range(18):
                hsl = xbcp[:, s_ * (Tc + 3):s_ * (Tc + 3) + 3]
                if seq_start:
                    P.op("v", lambda e, o=hsl: e.memset(o, 0.0))
                else:
                    P.op("v", lambda e, o=hsl, i=halob[:, s_ * 3:(s_ + 1) * 3]:
                         e.tensor_copy(out=o, in_=i))
            for s_ in range(18):
                P.op("v", lambda e, o=halob[:, s_ * 3:(s_ + 1) * 3],
                     i=xbcp[:, s_ * (Tc + 3) + Tc:(s_ + 1) * (Tc + 3)]:
                     e.tensor_copy(out=o, in_=i), wait=[inw_])
            for s_ in range(18):
                base = s_ * (Tc + 3)
                dst = xbco[:, s_ * Tc:(s_ + 1) * Tc]
                P.op("v", lambda e, o=dst, i0=xbcp[:, base:base + Tc],
                     w_=scvw[:, s_ * 4:s_ * 4 + 1],
                     bb=scvb[:, s_:s_ + 1].broadcast_to((128, Tc)):
                     e.scalar_tensor_tensor(out=o, in0=i0, scalar=w_, in1=bb,
                                            op0=ALU.mult, op1=ALU.add), wait=[ldw])
                for k in range(1, 4):
                    P.op("v", lambda e, o=dst, i0=xbcp[:, base + k:base + k + Tc],
                         w_=scvw[:, s_ * 4 + k:s_ * 4 + k + 1]:
                         e.scalar_tensor_tensor(out=o, in0=i0, scalar=w_, in1=o,
                                                op0=ALU.mult, op1=ALU.add))
                P.op("v", lambda e: e.engine_nop(), inc=[(cx.VV, 1)])
                P.op("s", lambda e, o=dst, i=dst: e.activation(
                     out=o, in_=i, func=AF.Silu), wait=[(cx.VV, cx.VV.n)],
                     inc=[(cx.SS, 1)])
            cvw_ = (cx.SS, cx.SS.n)
            P.op("s", lambda e, o=adt[0:32, 0:Tc], i=dtbuf[0:32, 0:Tc]:
                 e.activation(out=o, in_=i, func=AF.Exp, bias=sdtb[0:32, 0:1]),
                 wait=[inw_, ldw], inc=[(cx.SS, 1)])
            P.op("s", lambda e, o=dtbuf[0:32, 0:Tc], i=adt[0:32, 0:Tc]:
                 e.activation(out=o, in_=i, func=AF.Ln, bias=cx.cc("one", 32)),
                 inc=[(cx.SS, 1)])
            P.op("v", lambda e, o=adt[0:32, 0:Tc], i=dtbuf[0:32, 0:Tc]:
                 e.tensor_scalar(out=o, in0=i, scalar1=sacl[0:32, 0:1],
                                 scalar2=None, op0=ALU.mult),
                 wait=[(cx.SS, cx.SS.n), aw_], inc=[(cx.VV, 1)])

            if seq_start:
                P.op("v", lambda e: e.memset(Hbuf[:, :], 0.0),
                     wait=[(cx.TT, cx.TT.n)], inc=[(cx.VV, 1)])

            for c4 in range(2):
                co = c4 * 128
                P.op("v", lambda e, o=cumb[0:32, 0:128], d1=adt[0:32, co:co + 128],
                     d0=cx.cons[0:32, 0:1].broadcast_to((32, 128)):
                     e.tensor_tensor_scan(out=o, data0=d0, data1=d1, initial=0.0,
                                          op0=ALU.mult, op1=ALU.add),
                     wait=[(cx.VV, cx.VV.n)] + cx.const_wait, inc=[(cx.VV, 1)])
                for srcb, dstb in ((dtbuf, dt_t), (cumb, cum_t)):
                    bt = psum_acquire(cx)
                    soff = co if srcb is dtbuf else 0
                    P.op("t", lambda e, o=bt["h"][0:128, 0:32],
                         i=srcb[0:32, soff:soff + 128], idn=cx.ident2[0:32, 0:32]:
                         e.transpose(out=o, in_=i, identity=idn),
                         wait=bwait(bt) + [(cx.VV, cx.VV.n), cvw_, (cx.SS, cx.SS.n)],
                         inc=[(cx.TT, 1)])
                    P.op("v", lambda e, o=dstb[:, 0:32], i=bt["h"][0:128, 0:32]:
                         e.tensor_copy(out=o, in_=i), wait=[(cx.TT, cx.TT.n)],
                         inc=[(cx.VV, 1)])
                    bt["guard"] = (cx.VV, cx.VV.n)
                bE = psum_acquire(cx)
                P.op("t", lambda e, o=bE["h"][0:1, 0:32]: e.matmul(
                    out=o, lhsT=ssel[:, 0:1], rhs=cum_t[:, 0:32], start=True,
                    stop=True), wait=bwait(bE) + [(cx.VV, cx.VV.n), ldw],
                    inc=[(cx.TT, 1)])
                P.op("v", lambda e, o=cumb[0:1, 0:32], i=bE["h"][0:1, 0:32]:
                     e.tensor_copy(out=o, in_=i), wait=[(cx.TT, cx.TT.n)],
                     inc=[(cx.VV, 1)])
                bE["guard"] = (cx.VV, cx.VV.n)
                bE2 = psum_acquire(cx)
                P.op("t", lambda e, o=bE2["h"][:, 0:32]: e.matmul(
                    out=o, lhsT=cx.onesr, rhs=cumb[0:1, 0:32], start=True,
                    stop=True), wait=bwait(bE2) + [(cx.VV, cx.VV.n)],
                    inc=[(cx.TT, 1)])
                P.op("v", lambda e, o=ecole[:, 0:32], i=bE2["h"][:, 0:32]:
                     e.tensor_copy(out=o, in_=i), wait=[(cx.TT, cx.TT.n)],
                     inc=[(cx.VV, 1)])
                bE2["guard"] = (cx.VV, cx.VV.n)
                P.op("v", lambda e: e.tensor_sub(out=fmat[:, 0:32],
                                                 in0=ecole[:, 0:32],
                                                 in1=cum_t[:, 0:32]),
                     inc=[(cx.VV, 1)])
                P.op("s", lambda e: e.activation(out=fmat[:, 0:32],
                                                 in_=fmat[:, 0:32], func=AF.Exp),
                     wait=[(cx.VV, cx.VV.n)], inc=[(cx.SS, 1)])
                P.op("s", lambda e: e.activation(out=ecole[:, 0:32],
                                                 in_=ecole[:, 0:32], func=AF.Exp),
                     inc=[(cx.SS, 1)])
                P.op("s", lambda e: e.activation(out=ecumb[:, 0:32],
                                                 in_=cum_t[:, 0:32], func=AF.Exp),
                     inc=[(cx.SS, 1)])
                P.op("v", lambda e: e.tensor_mul(out=fmat[:, 0:32],
                                                 in0=fmat[:, 0:32],
                                                 in1=dt_t[:, 0:32]),
                     wait=[(cx.SS, cx.SS.n)], inc=[(cx.VV, 1)])
                Bsl = xbco[:, 16 * Tc + co:16 * Tc + co + 128]
                Csl = xbco[:, 17 * Tc + co:17 * Tc + co + 128]
                bS = psum_acquire(cx)
                P.op("t", lambda e, o=bS["h"][:, 0:128], l=Bsl, r=Csl: e.matmul(
                    out=o, lhsT=l, rhs=r, start=True, stop=True),
                    wait=bwait(bS) + [cvw_], inc=[(cx.TT, 1)])
                P.op("v", lambda e, o=stsb[:, :], i=bS["h"][:, 0:128]:
                     e.tensor_copy(out=o, in_=i), wait=[(cx.TT, cx.TT.n)],
                     inc=[(cx.VV, 1)])
                bS["guard"] = (cx.VV, cx.VV.n)
                bB = psum_acquire(cx)
                P.op("t", lambda e, o=bB["h"][:, 0:128], i=Bsl, idn=cx.identf[:, :]:
                     e.transpose(out=o, in_=i, identity=idn),
                     wait=bwait(bB), inc=[(cx.TT, 1)])
                P.op("v", lambda e, o=btsb[:, :], i=bB["h"][:, 0:128]:
                     e.tensor_copy(out=o, in_=i), wait=[(cx.TT, cx.TT.n)],
                     inc=[(cx.VV, 1)])
                bB["guard"] = (cx.VV, cx.VV.n)
                for s_ in range(16):
                    bx = psum_acquire(cx)
                    P.op("t", lambda e, o=bx["h"][:, 0:128],
                         i=xbco[:, s_ * Tc + co:s_ * Tc + co + 128],
                         idn=cx.identf[:, :]: e.transpose(out=o, in_=i, identity=idn),
                         wait=bwait(bx), inc=[(cx.TT, 1)])
                    P.op("v", lambda e, o=xt[:, s_ * 128:(s_ + 1) * 128],
                         i=bx["h"][:, 0:128]: e.tensor_copy(out=o, in_=i),
                         wait=[(cx.TT, cx.TT.n)], inc=[(cx.VV, 1)])
                    bx["guard"] = (cx.VV, cx.VV.n)
                P.op("v", lambda e: e.tensor_scalar_mul(
                     out=cumbn[0:32, 0:128], in0=cumb[0:32, 0:128], scalar1=-1.0),
                     inc=[(cx.VV, 1)])
                trw = (cx.VV, cx.VV.n)
                P.op("t", wait=[trw])
                for h in range(32):
                    bm = psum_acquire(cx)
                    P.op("t", lambda e, o=bm["h"][:, 0:128],
                         l=srsel[:, h * 128:(h + 1) * 128], r=cumb[0:32, 0:128]:
                         e.matmul(out=o, lhsT=l, rhs=r, start=True, stop=False),
                         wait=bwait(bm))
                    P.op("t", lambda e, o=bm["h"][:, 0:128], l=cumbn[0:32, 0:128],
                         r=srsel[:, h * 128:(h + 1) * 128]: e.matmul(
                         out=o, lhsT=l, rhs=r, start=False, stop=True),
                         inc=[(cx.TT, 1)])
                    tdm = (cx.TT, cx.TT.n)
                    P.op("v", lambda e, o=mtsb[:, :], i=bm["h"][:, 0:128]:
                         e.tensor_single_scalar(out=o, in_=i, scalar=0.0,
                                                op=ALU.min),
                         wait=[tdm, (cx.TT, cx.TT.n)], inc=[(cx.VV, 1)])
                    bm["guard"] = (cx.VV, cx.VV.n)
                    P.op("s", lambda e, o=mtsb[:, :], i=mtsb[:, :]: e.activation(
                         out=o, in_=i, func=AF.Exp), wait=[(cx.VV, cx.VV.n)],
                         inc=[(cx.SS, 1)])
                    P.op("v", lambda e, o=mtsb[:, :], sc=dt_t[:, h:h + 1]:
                         e.tensor_scalar(out=o, in0=o, scalar1=sc, scalar2=None,
                                         op0=ALU.mult), wait=[(cx.SS, cx.SS.n)])
                    P.op("v", lambda e, o=mtsb[:, :], i1=sutri[:, :]:
                         e.tensor_mul(out=o, in0=o, in1=i1))
                    P.op("v", lambda e, o=mtsb[:, :], i1=stsb[:, :]:
                         e.tensor_mul(out=o, in0=o, in1=i1), inc=[(cx.VV, 1)])
                    P.op("v", lambda e, o=bfsb[:, :], i0=btsb[:, :],
                         sc=fmat[:, h:h + 1]: e.tensor_scalar(
                         out=o, in0=i0, scalar1=sc, scalar2=None, op0=ALU.mult),
                         inc=[(cx.VV, 1)])
                    mw = (cx.VV, cx.VV.n)
                    by = psum_acquire(cx)
                    P.op("t", lambda e, o=by["h"][:, 0:64], l=mtsb[:, :],
                         r=xt[:, h * 64:(h + 1) * 64]: e.matmul(
                         out=o, lhsT=l, rhs=r, start=True, stop=True),
                         wait=bwait(by) + [mw], inc=[(cx.TT, 1)])
                    by2 = psum_acquire(cx)
                    P.op("t", lambda e, o=by2["h"][:, 0:64], l=Csl,
                         r=Hbuf[:, h * 64:(h + 1) * 64]: e.matmul(
                         out=o, lhsT=l, rhs=r, start=True, stop=True),
                         wait=bwait(by2), inc=[(cx.TT, 1)])
                    tmy = (cx.TT, cx.TT.n)
                    P.op("v", lambda e, o=ybuf[:, h * 64:(h + 1) * 64],
                         i=by["h"][:, 0:64]: e.tensor_copy(out=o, in_=i),
                         wait=[tmy], inc=[(cx.VV, 1)])
                    by["guard"] = (cx.VV, cx.VV.n)
                    P.op("v", lambda e, o=ybuf[:, h * 64:(h + 1) * 64],
                         i0=by2["h"][:, 0:64], sc=ecumb[:, h:h + 1],
                         i1=ybuf[:, h * 64:(h + 1) * 64]: e.scalar_tensor_tensor(
                         out=o, in0=i0, scalar=sc, in1=i1, op0=ALU.mult,
                         op1=ALU.add), inc=[(cx.VV, 1)])
                    by2["guard"] = (cx.VV, cx.VV.n)
                    P.op("v", lambda e, o=ybuf[:, h * 64:(h + 1) * 64],
                         i0=xt[:, h * 64:(h + 1) * 64], sc=dcolb[:, h:h + 1],
                         i1=ybuf[:, h * 64:(h + 1) * 64]: e.scalar_tensor_tensor(
                         out=o, in0=i0, scalar=sc, in1=i1, op0=ALU.mult,
                         op1=ALU.add))
                    bU = psum_acquire(cx)
                    P.op("t", lambda e, o=bU["h"][:, 0:64], l=bfsb[:, :],
                         r=xt[:, h * 64:(h + 1) * 64]: e.matmul(
                         out=o, lhsT=l, rhs=r, start=True, stop=True),
                         wait=bwait(bU), inc=[(cx.TT, 1)])
                    tmu = (cx.TT, cx.TT.n)
                    P.op("v", lambda e, o=Hbuf[:, h * 64:(h + 1) * 64],
                         sc=ecole[:, h:h + 1]: e.tensor_scalar(
                         out=o, in0=o, scalar1=sc, scalar2=None, op0=ALU.mult),
                         wait=[tmu])
                    P.op("v", lambda e, o=Hbuf[:, h * 64:(h + 1) * 64],
                         i0=Hbuf[:, h * 64:(h + 1) * 64], i1=bU["h"][:, 0:64]:
                         e.tensor_add(out=o, in0=i0, in1=i1), inc=[(cx.VV, 1)])
                    bU["guard"] = (cx.VV, cx.VV.n)
                yw = (cx.VV, cx.VV.n)
                P.op("t", wait=[yw])
                for s_ in range(16):
                    bx = psum_acquire(cx)
                    P.op("t", lambda e, o=bx["h"][:, 0:128],
                         i=ybuf[:, s_ * 128:(s_ + 1) * 128], idn=cx.identf[:, :]:
                         e.transpose(out=o, in_=i, identity=idn),
                         wait=bwait(bx), inc=[(cx.TT, 1)])
                    P.op("v", lambda e, o=yfm[:, s_ * Tc + co:s_ * Tc + co + 128],
                         i=bx["h"][:, 0:128]: e.tensor_copy(out=o, in_=i),
                         wait=[(cx.TT, cx.TT.n)], inc=[(cx.VV, 1)])
                    bx["guard"] = (cx.VV, cx.VV.n)

            for s_ in range(16):
                P.op("s", lambda e, o=zbuf[:, s_ * Tc:(s_ + 1) * Tc],
                     i=zbuf[:, s_ * Tc:(s_ + 1) * Tc]: e.activation(
                     out=o, in_=i, func=AF.Silu), wait=[(cx.VV, cx.VV.n)],
                     inc=[(cx.SS, 1)])
            P.op("v", wait=[(cx.SS, cx.SS.n)])
            for s_ in range(16):
                P.op("v", lambda e, o=yfm[:, s_ * Tc:(s_ + 1) * Tc],
                     i1=zbuf[:, s_ * Tc:(s_ + 1) * Tc]: e.tensor_mul(
                     out=o, in0=o, in1=i1),
                     inc=([(cx.VV, 1)] if s_ == 15 else []))
            bq = psum_acquire(cx)
            bqw2 = bwait(bq)
            for s_ in range(16):
                sqv = cx.sq[:, (s_ % 2) * Tc:(s_ % 2) * Tc + Tc]
                P.op("s", lambda e, o=sqv, r=yfm[:, s_ * Tc:(s_ + 1) * Tc]:
                     e.activation(out=o, in_=r, func=AF.Square),
                     wait=[(cx.VV, cx.VV.n)] +
                     ([(cx.TT, cx.sq_guard[s_ % 2])] if cx.sq_guard[s_ % 2] else []),
                     inc=[(cx.SS, 1)])
                P.op("t", lambda e, o=bq["h"][0:1, :Tc], r=sqv, st=(s_ == 0),
                     sp=(s_ == 15): e.matmul(out=o, lhsT=cx.ones_col, rhs=r,
                                             start=st, stop=sp),
                     wait=[(cx.SS, cx.SS.n)] + (bqw2 if s_ == 0 else []),
                     inc=[(cx.TT, 1)])
                cx.sq_guard[s_ % 2] = cx.TT.n
            st_ = cx.stats
            ms = st_[0:1, 0:Tc]
            P.op("v", lambda e, o=ms, i=bq["h"][0:1, :Tc]: e.tensor_scalar_mul(
                 out=o, in0=i, scalar1=1.0 / 2048.0),
                 wait=[(cx.TT, cx.TT.n)], inc=[(cx.VV, 1)])
            bq["guard"] = (cx.VV, cx.VV.n)
            P.op("s", lambda e, o=ms, i=ms: e.activation(
                 out=o, in_=i, func=AF.Sqrt, bias=cx.cc("eps1e5", 1)),
                 wait=[(cx.VV, cx.VV.n)], inc=[(cx.SS, 1)])
            P.op("v", lambda e, o=ms, i=ms: e.reciprocal(out=o, in_=i),
                 wait=[(cx.SS, cx.SS.n)], inc=[(cx.VV, 1)])
            bR = psum_acquire(cx)
            P.op("t", lambda e, o=bR["h"][:, :Tc]: e.matmul(
                 out=o, lhsT=cx.onesr, rhs=ms, start=True, stop=True),
                 wait=bwait(bR) + [(cx.VV, cx.VV.n)], inc=[(cx.TT, 1)])
            trs = (cx.TT, cx.TT.n)
            for s_ in range(16):
                P.op("v", lambda e, o=yfm[:, s_ * Tc:(s_ + 1) * Tc],
                     i1=bR["h"][:, :Tc]: e.tensor_mul(out=o, in0=o, in1=i1),
                     wait=[trs])
                P.op("v", lambda e, o=yfm[:, s_ * Tc:(s_ + 1) * Tc],
                     sc=srmw[:, s_:s_ + 1]: e.tensor_scalar(
                     out=o, in0=o, scalar1=sc, scalar2=None, op0=ALU.mult),
                     inc=[(cx.VV, 1)])
            bR["guard"] = (cx.VV, cx.VV.n)
            P.op("t", wait=[(cx.VV, cx.VV.n)])

            yst = [yfm[:, s_ * Tc:(s_ + 1) * Tc] for s_ in range(16)]

            def evac_out(m, b, tmm):
                P.op("v", lambda e, o=midb[:, m * Tc:(m + 1) * Tc],
                     i0=b["h"][:, :Tc], i1=xn[m]: e.tensor_add(
                     out=o, in0=i0, in1=i1), wait=[tmm], inc=[(cx.VV, 1)])
                b["guard"] = (cx.VV, cx.VV.n)
            emit_linear(cx, outw, 2048, 1024, yst, Tc, evac_out)
            mids = [midb[:, i * Tc:(i + 1) * Tc] for i in range(NSTR)]
            a2s = [a2b[:, i * Tc:(i + 1) * Tc] for i in range(NSTR)]
            P.op("t", wait=[(cx.VV, cx.VV.n)])
            P.op("s", wait=[(cx.VV, cx.VV.n)])
            ln3 = emit_ln(cx, mids, a2s, Tc, NSTR, gam=gams, bet=bets)
            P.op("t", wait=[ln3])
            for t2 in range(4):
                co2 = t2 * 64
                a2h = [a2b[:, i * Tc + co2:i * Tc + co2 + 64] for i in range(NSTR)]

                def evac_g(m, b, tmm):
                    P.op("s", lambda e, o=hmlp[:, m * 64:(m + 1) * 64],
                         i=b["h"][:, :64], bi=sb1[:, m:m + 1]: e.activation(
                         out=o, in_=i, func=AF.Relu, bias=bi),
                         wait=[tmm, ldw, (cx.TT, cx.TT.n)], inc=[(cx.SS, 1)])
                    b["guard"] = (cx.SS, cx.SS.n)
                emit_linear(cx, w1, 1024, 4096, a2h, 64, evac_g)
                P.op("t", wait=[(cx.SS, cx.SS.n)])
                hst = [hmlp[:, i * 64:(i + 1) * 64] for i in range(32)]

                def evac_o2(m, b, tmm, co2=co2):
                    P.op("v", lambda e, o=cx.sq[:, 0:64],
                         i0=b["h"][:, :64], bc=sb2[:, m:m + 1],
                         i1=midb[:, m * Tc + co2:m * Tc + co2 + 64]:
                         e.scalar_tensor_tensor(
                         out=o, in0=i0, scalar=bc, in1=i1, op0=ALU.add,
                         op1=ALU.add), wait=[tmm, (cx.SS, cx.SS.n)],
                         inc=[(cx.VV, 1)])
                    b["guard"] = (cx.VV, cx.VV.n)
                    P.op("v", lambda e, o=cx.xbuf[:, m * Tc + co2:m * Tc + co2 + 64],
                         i0=cx.sq[:, 0:64],
                         i1=cx.xbuf[:, m * Tc + co2:m * Tc + co2 + 64]:
                         e.tensor_add(out=o, in0=i0, in1=i1),
                         inc=[(cx.VV, 1)])
                emit_linear(cx, w2, 4096, 1024, hst, 64, evac_o2)
            store_x_chunk_g(cx, c, cx.xbuf, xdst, Tc, wait=[(cx.VV, cx.VV.n)])
    cx.xsrc = xdst





def emit_l0f(cx):
    P, nc = cx.P, cx.nc
    Tc = 512
    nch = T // Tc
    gw = cx.win("l0f_gw", [128, 64])
    wge = cx.win("l0f_wg", [8, 16, 1, 128, KU])
    wue = cx.win("l0f_wu", [8, 16, 1, 128, KU])
    wde = cx.win("l0f_wd", [8, 8, 2, 128, KU])
    pfx = cx.win("l0f_pfx", [128, 16])
    xnd = nc.dram_tensor("l0f_xnd", [D, T], F32, kind="Internal").ap()
    combd = nc.dram_tensor("l0f_comb", [8, T], F32, kind="Internal").ap()
    ccin = nc.dram_tensor("l0f_ccin", [16, 1], F32, kind="Internal").ap()
    ccout = nc.dram_tensor("l0f_ccout", [128, 1], F32, kind="Internal",
                           addr_space="Shared").ap()
    last = cx.cur_block_last

    with contextlib.ExitStack() as les:
        sgw = les.enter_context(nc.sbuf_tensor("l0f_sgw", [128, 64], F32))
        spfx = les.enter_context(nc.sbuf_tensor("l0f_spfx", [128, 16], F32))
        ohR = les.enter_context(nc.sbuf_tensor("l0f_ohR", [16, T], F32))
        obh = les.enter_context(nc.sbuf_tensor("l0f_obh", [128, 32 * 16], F32))
        cbuf = les.enter_context(nc.sbuf_tensor("l0f_cb", [128, 64], F32))
        pb_ = les.enter_context(nc.sbuf_tensor("l0f_pb", [128, 8], F32))
        t8a = les.enter_context(nc.sbuf_tensor("l0f_t8a", [128, 8], F32))
        t8b = les.enter_context(nc.sbuf_tensor("l0f_t8b", [128, 8], F32))
        t8c = les.enter_context(nc.sbuf_tensor("l0f_t8c", [128, 16], F32))
        ccs = les.enter_context(nc.sbuf_tensor("l0f_ccs", [128, 1], F32))
        offc = les.enter_context(nc.sbuf_tensor("l0f_off", [16, 1], F32))
        hbuf = les.enter_context(nc.sbuf_tensor("l0f_h", [128, 16 * Tc], F32))
        outa = les.enter_context(nc.sbuf_tensor("l0f_out", [128, 8 * Tc], F32))
        ctile = les.enter_context(nc.sbuf_tensor("l0f_ct", [128, Tc], F32))
        crow = les.enter_context(nc.sbuf_tensor("l0f_cr", [1, Tc], F32))

        ldw = const_dma(cx, lambda e: e.dma_start(out=sgw[:, :], in_=gw))
        ldw = const_dma(cx, lambda e: e.dma_start(out=spfx[:, :], in_=pfx))

        xdst = cx.xdst_for(last)
        st_ = cx.stats
        for c in range(nch):
            lw = load_x_chunk_g(cx, c, Tc)
            xs = [cx.xbuf[:, i * Tc:(i + 1) * Tc] for i in range(NSTR)]
            xn = [cx.xnbuf[:, i * Tc:(i + 1) * Tc] for i in range(NSTR)]
            P.op("v", wait=[lw])
            P.op("s", wait=[lw])
            P.op("t", wait=[lw])
            lnw = emit_ln(cx, xs, xn, Tc, NSTR)
            prev = cx.LX.n
            P.op("g", lambda e, s=cx.xnbuf[:, 0:NSTR * Tc].rearrange(
                 "p (s c) -> p s c", c=Tc),
                 d=xnd[:, c * Tc:(c + 1) * Tc].rearrange("(s p) c -> p s c", p=128):
                 e.dma_start(out=d, in_=s),
                 wait=[lnw] + ([(cx.LX, prev)] if prev else []), inc=[(cx.LX, 16)])
            P.op("t", wait=[lnw])
            for tt in range(4):
                col = tt * 128
                gtile = c * 4 + tt
                bl = psum_acquire(cx)
                for k in range(8):
                    P.op("t", lambda e, o=bl["h"][:, 0:8],
                         l=cx.xnbuf[:, k * Tc + col:k * Tc + col + 128],
                         r=sgw[:, k * 8:(k + 1) * 8], st=(k == 0), sp=(k == 7):
                         e.matmul(out=o, lhsT=l, rhs=r, start=st, stop=sp),
                         wait=(bwait(bl) if k == 0 else []) + [ldw],
                         inc=([(cx.TT, 1)] if k == 7 else []))
                tmm = (cx.TT, cx.TT.n)
                rmax = st_[0:1, 7 * TC:7 * TC + 1]
                P.op("v", lambda e, o=t8c[:, 0:1], i=bl["h"][:, 0:8]:
                     e.reduce_max(out=o, in_=i, axis=AX.X, negate=True),
                     wait=[tmm, (cx.SS, cx.SS.n)], inc=[(cx.VV, 1)])
                P.op("s", lambda e, o=pb_[:, :], i=bl["h"][:, 0:8]: e.activation(
                     out=o, in_=i, func=AF.Exp, bias=t8c[:, 0:1],
                     accum_out=t8c[:, 1:2]), wait=[(cx.VV, cx.VV.n)],
                     inc=[(cx.SS, 1)])
                bl["guard"] = (cx.SS, cx.SS.n)
                P.op("v", lambda e: e.reciprocal(out=t8c[:, 2:3], in_=t8c[:, 1:2]),
                     wait=[(cx.SS, cx.SS.n)])
                P.op("v", lambda e, o=pb_[:, :]: e.tensor_scalar(
                     out=o, in0=o, scalar1=t8c[:, 2:3], scalar2=None, op0=ALU.mult))
                P.op("v", lambda e: e.reduce_max(out=t8c[:, 3:4], in_=pb_[:, :],
                                                 axis=AX.X))
                P.op("v", lambda e, o=t8a[:, :], i=pb_[:, :]: e.tensor_scalar(
                     out=o, in0=i, scalar1=t8c[:, 3:4], scalar2=None, op0=ALU.is_ge))
                P.op("v", lambda e, o=t8b[:, :], i=t8a[:, :],
                     d0=cx.cons[:, 0:1].broadcast_to((128, 8)):
                     e.tensor_tensor_scan(out=o, data0=d0, data1=i, initial=0.0,
                                          op0=ALU.mult, op1=ALU.add))
                P.op("v", lambda e, o=t8b[:, :], i=t8b[:, :]: e.tensor_single_scalar(
                     out=o, in_=i, scalar=1.0, op=ALU.is_equal))
                P.op("v", lambda e, o=t8a[:, :], i0=t8a[:, :], i1=t8b[:, :]:
                     e.tensor_mul(out=o, in0=i0, in1=i1))
                P.op("v", lambda e, o=t8b[:, :], i=t8a[:, :]: e.tensor_scalar(
                     out=o, in0=i, scalar1=-1.0, scalar2=1.0, op0=ALU.mult,
                     op1=ALU.add))
                P.op("v", lambda e, o=t8b[:, :], i0=pb_[:, :], i1=t8b[:, :]:
                     e.tensor_mul(out=o, in0=i0, in1=i1))
                P.op("v", lambda e: e.reduce_max(out=t8c[:, 4:5], in_=t8b[:, :],
                                                 axis=AX.X))
                P.op("v", lambda e, o=t8b[:, :], i=t8b[:, :]: e.tensor_scalar(
                     out=o, in0=i, scalar1=t8c[:, 4:5], scalar2=None, op0=ALU.is_ge))
                P.op("v", lambda e, o=pb_[:, :], i=t8b[:, :],
                     d0=cx.cons[:, 0:1].broadcast_to((128, 8)):
                     e.tensor_tensor_scan(out=o, data0=d0, data1=i, initial=0.0,
                                          op0=ALU.mult, op1=ALU.add))
                P.op("v", lambda e, o=pb_[:, :], i=pb_[:, :]: e.tensor_single_scalar(
                     out=o, in_=i, scalar=1.0, op=ALU.is_equal))
                P.op("v", lambda e, o=t8b[:, :], i0=t8b[:, :], i1=pb_[:, :]:
                     e.tensor_mul(out=o, in0=i0, in1=i1))
                P.op("v", lambda e: e.tensor_add(out=t8c[:, 5:6], in0=t8c[:, 3:4],
                                                 in1=t8c[:, 4:5]))
                P.op("v", lambda e, o=t8c[:, 5:6]: e.tensor_scalar_add(
                     out=o, in0=o, scalar1=1e-9))
                P.op("v", lambda e: e.reciprocal(out=t8c[:, 6:7], in_=t8c[:, 5:6]))
                P.op("v", lambda e, o=cbuf[:, gtile:gtile + 1], i0=t8c[:, 3:4],
                     i1=t8c[:, 6:7]: e.tensor_mul(out=o, in0=i0, in1=i1))
                P.op("v", lambda e, o=cbuf[:, 32 + gtile:33 + gtile],
                     i0=t8c[:, 4:5], i1=t8c[:, 6:7]: e.tensor_mul(
                     out=o, in0=i0, in1=i1))
                P.op("v", lambda e, o=obh[:, gtile * 16:gtile * 16 + 8],
                     i=t8a[:, :]: e.tensor_copy(out=o, in_=i))
                P.op("v", lambda e, o=obh[:, gtile * 16 + 8:gtile * 16 + 16],
                     i=t8b[:, :]: e.tensor_copy(out=o, in_=i), inc=[(cx.VV, 1)])
                bt = psum_acquire(cx)
                P.op("t", lambda e, o=bt["h"][0:16, 0:128],
                     i=obh[:, gtile * 16:gtile * 16 + 16], idn=cx.identf[:, :]:
                     e.transpose(out=o, in_=i, identity=idn),
                     wait=bwait(bt) + [(cx.VV, cx.VV.n)], inc=[(cx.TT, 1)])
                P.op("v", lambda e, o=ohR[0:16, gtile * 128:(gtile + 1) * 128],
                     i=bt["h"][0:16, 0:128]: e.tensor_copy(out=o, in_=i),
                     wait=[(cx.TT, cx.TT.n)], inc=[(cx.VV, 1)])
                bt["guard"] = (cx.VV, cx.VV.n)
        P.op("v", lambda e, d0=cx.cons[0:16, 0:1].broadcast_to((16, T)):
             e.tensor_tensor_scan(out=ohR[0:16, :], data0=d0, data1=ohR[0:16, :],
                                  initial=0.0, op0=ALU.mult, op1=ALU.add),
             inc=[(cx.VV, 1)])
        prev = cx.LX.n
        P.op("g", lambda e, s=ohR[0:16, T - 1:T], d=ccin: e.dma_start(
             out=d, in_=s), wait=[(cx.VV, cx.VV.n)] + ([(cx.LX, prev)] if prev else []),
             inc=[(cx.LX, 16)])
        P.op("g", lambda e: e.collective_compute(
             "AllGather", ALU.bypass, ins=[ccin], outs=[ccout],
             replica_groups=[list(range(8))]),
             wait=[(cx.LX, cx.LX.n)], inc=[(cx.CC, 1)])
        prev = cx.LX.n
        P.op("g", lambda e, s=ccout, d=ccs[:, :]: e.dma_start(out=d, in_=s),
             wait=[(cx.CC, cx.CC.n), (cx.LX, prev)], inc=[(cx.LX, 16)])
        bo = psum_acquire(cx)
        P.op("t", lambda e, o=bo["h"][0:16, 0:1]: e.matmul(
             out=o, lhsT=spfx[:, :], rhs=ccs[:, :], start=True, stop=True),
             wait=bwait(bo) + [(cx.LX, cx.LX.n), ldw], inc=[(cx.TT, 1)])
        P.op("v", lambda e, o=offc[:, :], i=bo["h"][0:16, 0:1]: e.tensor_copy(
             out=o, in_=i), wait=[(cx.TT, cx.TT.n)], inc=[(cx.VV, 1)])
        bo["guard"] = (cx.VV, cx.VV.n)
        P.op("v", lambda e, o=ohR[0:16, :], i=ohR[0:16, :]: e.tensor_scalar(
             out=o, in0=i, scalar1=offc[:, 0:1], scalar2=5121.0, op0=ALU.add,
             op1=ALU.is_lt), inc=[(cx.VV, 1)])
        kpw = (cx.VV, cx.VV.n)

        for c in range(nch):
            for tt in range(4):
                gtile = c * 4 + tt
                bt = psum_acquire(cx)
                P.op("t", lambda e, o=bt["h"][0:128, 0:16],
                     i=ohR[0:16, gtile * 128:(gtile + 1) * 128],
                     idn=cx.ident2[0:16, 0:16]: e.transpose(
                     out=o, in_=i, identity=idn), wait=bwait(bt) + [kpw],
                     inc=[(cx.TT, 1)])
                tmm = (cx.TT, cx.TT.n)
                P.op("v", lambda e, o=t8c[:, 0:16], i=bt["h"][0:128, 0:16]:
                     e.tensor_copy(out=o, in_=i), wait=[tmm, (cx.SS, cx.SS.n)],
                     inc=[(cx.VV, 1)])
                bt["guard"] = (cx.VV, cx.VV.n)
                P.op("v", lambda e, o=t8a[:, :], i0=obh[:, gtile * 16:gtile * 16 + 8],
                     i1=t8c[:, 0:8]: e.tensor_mul(out=o, in0=i0, in1=i1))
                P.op("v", lambda e, o=t8c[:, 0:1], i=t8a[:, :]: e.tensor_reduce(
                     out=o, in_=i, axis=AX.X, op=ALU.add))
                P.op("v", lambda e, o=t8b[:, :],
                     i0=obh[:, gtile * 16 + 8:gtile * 16 + 16], i1=t8c[:, 8:16]:
                     e.tensor_mul(out=o, in0=i0, in1=i1))
                P.op("v", lambda e, o=t8c[:, 1:2], i=t8b[:, :]: e.tensor_reduce(
                     out=o, in_=i, axis=AX.X, op=ALU.add))
                P.op("v", lambda e, o=t8c[:, 2:3], i0=cbuf[:, gtile:gtile + 1],
                     i1=t8c[:, 0:1]: e.tensor_mul(out=o, in0=i0, in1=i1))
                P.op("v", lambda e, o=t8c[:, 3:4],
                     i0=cbuf[:, 32 + gtile:33 + gtile], i1=t8c[:, 1:2]:
                     e.tensor_mul(out=o, in0=i0, in1=i1))
                P.op("v", lambda e, o=t8a[:, :],
                     i0=obh[:, gtile * 16:gtile * 16 + 8]: e.tensor_scalar(
                     out=o, in0=i0, scalar1=t8c[:, 2:3], scalar2=None,
                     op0=ALU.mult))
                P.op("v", lambda e, o=t8b[:, :],
                     i0=obh[:, gtile * 16 + 8:gtile * 16 + 16]: e.tensor_scalar(
                     out=o, in0=i0, scalar1=t8c[:, 3:4], scalar2=None,
                     op0=ALU.mult))
                P.op("v", lambda e, o=t8a[:, :], i0=t8a[:, :], i1=t8b[:, :]:
                     e.tensor_add(out=o, in0=i0, in1=i1), inc=[(cx.VV, 1)])
                bt2 = psum_acquire(cx)
                P.op("t", lambda e, o=bt2["h"][0:8, 0:128], i=t8a[:, 0:8],
                     idn=cx.identf[:, :]: e.transpose(out=o, in_=i, identity=idn),
                     wait=bwait(bt2) + [(cx.VV, cx.VV.n)], inc=[(cx.TT, 1)])
                tmm = (cx.TT, cx.TT.n)
                P.op("v", lambda e, o=hbuf[0:8, tt * 128:(tt + 1) * 128],
                     i=bt2["h"][0:8, 0:128]: e.tensor_copy(out=o, in_=i),
                     wait=[tmm], inc=[(cx.VV, 1)])
                bt2["guard"] = (cx.VV, cx.VV.n)
            prev = cx.LX.n
            P.op("g", lambda e, s=hbuf[0:8, 0:512],
                 d=combd[:, c * Tc:(c + 1) * Tc]: e.dma_start(out=d, in_=s),
                 wait=[(cx.VV, cx.VV.n)] + ([(cx.LX, prev)] if prev else []),
                 inc=[(cx.LX, 16)])

        for c in range(nch):
            lw = load_x_chunk_g(cx, c, Tc)
            xs = [cx.xbuf[:, i * Tc:(i + 1) * Tc] for i in range(NSTR)]
            prev = cx.LX.n
            P.op("g", lambda e, s=xnd[:, c * Tc:(c + 1) * Tc].rearrange(
                 "(s p) c -> p s c", p=128),
                 d=cx.xnbuf[:, 0:NSTR * Tc].rearrange("p (s c) -> p s c", c=Tc):
                 e.dma_start(out=d, in_=s), wait=[(cx.LX, prev)],
                 inc=[(cx.LX, 16)])
            xnw = (cx.LX, cx.LX.n)
            xn = [cx.xnbuf[:, i * Tc:(i + 1) * Tc] for i in range(NSTR)]
            P.op("t", wait=[xnw])
            for e_ in range(8):
                prev = cx.LX.n
                P.op("g", lambda e, s=combd[e_:e_ + 1, c * Tc:(c + 1) * Tc],
                     d=crow[:, :]: e.dma_start(out=d, in_=s),
                     wait=[(cx.LX, prev), (cx.VV, cx.VV.n), (cx.TT, cx.TT.n)],
                     inc=[(cx.LX, 16)])
                bc = psum_acquire(cx)
                P.op("t", lambda e, o=bc["h"][:, :Tc]: e.matmul(
                     out=o, lhsT=cx.onesr, rhs=crow[:, :], start=True, stop=True),
                     wait=bwait(bc) + [(cx.LX, cx.LX.n)] + cx.const_wait,
                     inc=[(cx.TT, 1)])
                P.op("v", lambda e, o=ctile[:, :], i=bc["h"][:, :Tc]:
                     e.tensor_copy(out=o, in_=i), wait=[(cx.TT, cx.TT.n)],
                     inc=[(cx.VV, 1)])
                bc["guard"] = (cx.VV, cx.VV.n)
                cw_ = (cx.VV, cx.VV.n)
                for m in range(16):
                    bg = psum_acquire(cx)
                    bu = psum_acquire(cx)
                    for which, (wdr, bk) in enumerate(((wge, bg), (wue, bu))):
                        slot, wld = stream_unit(cx, wdr[e_, m, 0])
                        for k in range(8):
                            P.op("t", lambda e, o=bk["h"][:, :Tc],
                                 l=slot[:, k * 128:(k + 1) * 128],
                                 r=xn[k], st=(k == 0), sp=(k == 7): e.matmul(
                                 out=o, lhsT=l, rhs=r, start=st, stop=sp),
                                 wait=([wld] if k == 0 else []) +
                                      (bwait(bk) if k == 0 else []),
                                 inc=([(cx.w_free, 1), (cx.TT, 1)] if k == 7 else []))
                    tmm = (cx.TT, cx.TT.n)
                    hst = hbuf[:, m * Tc:(m + 1) * Tc]
                    P.op("s", lambda e, o=cx.sq[:, 0:Tc], i=bg["h"][:, :Tc]:
                         e.activation(out=o, in_=i, func=AF.Silu),
                         wait=[tmm, (cx.VV, cx.VV.n)], inc=[(cx.SS, 1)])
                    P.op("v", lambda e, o=hst, i0=cx.sq[:, 0:Tc],
                         i1=bu["h"][:, :Tc]: e.tensor_mul(out=o, in0=i0, in1=i1),
                         wait=[(cx.SS, cx.SS.n), cw_])
                    P.op("v", lambda e, o=hst, i1=ctile[:, :]: e.tensor_mul(
                         out=o, in0=o, in1=i1), inc=[(cx.VV, 1)])
                    bg["guard"] = (cx.VV, cx.VV.n)
                    bu["guard"] = (cx.VV, cx.VV.n)
                hw_ = (cx.VV, cx.VV.n)
                P.op("t", wait=[hw_])
                hsts = [hbuf[:, i * Tc:(i + 1) * Tc] for i in range(16)]

                def evac_d(m, b, tmm, e_=e_):
                    if e_ == 0:
                        P.op("v", lambda e, o=outa[:, m * Tc:(m + 1) * Tc],
                             i=b["h"][:, :Tc]: e.tensor_copy(out=o, in_=i),
                             wait=[tmm], inc=[(cx.VV, 1)])
                    else:
                        P.op("v", lambda e, o=outa[:, m * Tc:(m + 1) * Tc],
                             i0=outa[:, m * Tc:(m + 1) * Tc], i1=b["h"][:, :Tc]:
                             e.tensor_add(out=o, in0=i0, in1=i1),
                             wait=[tmm], inc=[(cx.VV, 1)])
                    b["guard"] = (cx.VV, cx.VV.n)
                emit_linear(cx, wde[e_], 2048, 1024, hsts, Tc, evac_d)
            for m in range(NSTR):
                P.op("v", lambda e, o=xs[m], i0=outa[:, m * Tc:(m + 1) * Tc],
                     i1=xs[m]: e.tensor_add(out=o, in0=i0, in1=i1),
                     wait=([lw] if m == 0 else []), inc=[(cx.VV, 1)])
            store_x_chunk_g(cx, c, cx.xbuf, xdst, Tc, wait=[(cx.VV, cx.VV.n)])
    cx.xsrc = xdst



EMITTERS = {"l3f": emit_l3f, "l1f": emit_l1f, "l2f": emit_l2f,
            "l3a": emit_l3a, "l1a": emit_l1a, "l2a": emit_l2a,
            "l0a": emit_l0a, "l0f": emit_l0f}



def build_program(blocks=None):
    blocks = list(ALL_BLOCKS) if blocks is None else list(blocks)
    nc = bass.Bass("TRN2", target_bir_lowering=False, debug=False)
    cx = Ctx()
    cx.nc = nc
    cx.P = Prog()
    P = cx.P

    xin = nc.dram_tensor("xin", [D, T], F32, kind="ExternalInput").ap()
    xout = nc.dram_tensor("xout", [D, T], F32, kind="ExternalOutput").ap()
    xcur = nc.dram_tensor("xcur", [D, T], F32, kind="Internal").ap()
    cx.xsrc = xin
    cx.xdst_for = lambda is_last: xout if is_last else xcur
    cx.wnames = []

    def win(name, shape, dtype=F32):
        cx.wnames.append(name)
        return nc.dram_tensor(name, list(shape), dtype, kind="ExternalInput").ap()
    cx.win = win

    es = contextlib.ExitStack()
    cx.es = es
    with es:
        cx.xbuf = es.enter_context(nc.sbuf_tensor("xbuf", [128, NSTR * TC], F32))
        cx.xnbuf = es.enter_context(nc.sbuf_tensor("xnbuf", [128, NSTR * TC], F32))
        cx.ring = es.enter_context(nc.sbuf_tensor("ring", [128, RING_SLOTS * KU], F32))
        cx.sq = es.enter_context(nc.sbuf_tensor("sq", [128, 2 * TC], F32))
        cx.stats = es.enter_context(nc.sbuf_tensor("stats", [1, 8 * TC], F32))
        cx.cons = es.enter_context(nc.sbuf_tensor("cons", [128, NCONS], F32))
        cx.cc = lambda name, p=128: cx.cons[0:p, CONST_COLS[name]:CONST_COLS[name] + 1]
        onesrt = es.enter_context(nc.sbuf_tensor("onesr", [128, 128], F32))
        identt = es.enter_context(nc.sbuf_tensor("ident2", [128, 64], F32))
        cx.ident2 = identt
        identf = es.enter_context(nc.sbuf_tensor("identf", [128, 128], F32))
        cx.identf = identf
        cx.onesr = onesrt[0:1, :]
        cx.onesr64 = onesrt[64:65, :]
        cx.banks = []
        for i in range(6):
            h = es.enter_context(nc.psum_tensor(f"pb{i}", [128, 512], F32))
            cx.banks.append({"h": h, "guard": None})
        cx.bank_rr = 0
        cx.sq_guard = [0, 0]

        for nm in ("w_free", "LX", "TT", "VV", "SS", "GG", "YY", "CONST", "CC"):
            setattr(cx, nm, Sem(es.enter_context(nc.semaphore(nm))))
        cx.wslot = [Sem(es.enter_context(nc.semaphore(f"ws{j}")))
                    for j in range(RING_SLOTS)]
        cx.slot_uses = [0] * RING_SLOTS
        cx.woslot = [Sem(es.enter_context(nc.semaphore(f"wo{j}"))) for j in range(2)]
        cx.wofree = Sem(es.enter_context(nc.semaphore("wofree")))
        cx.wo_uses = [0, 0]
        cx.wo_id = 0
        cx.ktguard = [None, None]

        open_const_group(cx, "cg_base")
        cdram = win("cons_in", [128, NCONS])
        ordram = win("onesr_in", [128, 128])
        iddram = win("ident2_in", [128, 64])
        idfdram = win("identf_in", [128, 128])
        const_dma(cx, lambda e: e.dma_start(out=cx.cons[:, :], in_=cdram))
        const_dma(cx, lambda e: e.dma_start(out=onesrt[:, :], in_=ordram))
        const_dma(cx, lambda e: e.dma_start(out=identt[:, :], in_=iddram))
        const_dma(cx, lambda e: e.dma_start(out=identf[:, :], in_=idfdram))
        cx.ones_col = cx.cons[:, 0:1]
        cx.const_wait = [(cx.cgroup, None)]
        cx.slab_id = 0

        for bi, bname in enumerate(blocks):
            cx.cur_block_last = (bi == len(blocks) - 1)
            open_const_group(cx, f"cg_{bname}_{bi}")
            EMITTERS[bname](cx)

        P.op("y", wait=[(cx.LX, cx.LX.n)])
        P.op("g", wait=[(cx.LX, cx.LX.n)])

        with nc.Block() as block:
            @block.tensor
            def _(eng):
                for f in P.q["t"]:
                    f(eng)

            @block.vector
            def _(eng):
                for f in P.q["v"]:
                    f(eng)

            @block.scalar
            def _(eng):
                for f in P.q["s"]:
                    f(eng)

            @block.gpsimd
            def _(eng):
                for f in P.q["g"]:
                    f(eng)

            @block.sync
            def _(eng):
                for f in P.q["y"]:
                    f(eng)
    return nc



def prep_in_maps(inputs, blocks=None):
    blocks = list(ALL_BLOCKS) if blocks is None else list(blocks)
    common = {}
    cons = np.zeros((128, NCONS), dtype=np.float32)
    for nm, col in CONST_COLS.items():
        cons[:, col] = CONST_VALS[nm]
    common["cons_in"] = cons
    common["onesr_in"] = np.ones((128, 128), dtype=np.float32)
    id2 = np.zeros((128, 64), dtype=np.float32)
    for r in range(128):
        id2[r, r % 64] = 1.0
    common["ident2_in"] = id2
    common["identf_in"] = np.eye(128, dtype=np.float32)

    if "l3f" in blocks:
        w8 = np.asarray(inputs["L3_ffn_weight"])
        w8t = np.ascontiguousarray(w8.T)
        common["l3f_w"] = _prep_w(w8t.view(np.uint8))
        common["l3f_wsc"] = np.asarray(inputs["L3_ffn_weight_scale"],
                                       dtype=np.float32).reshape(1, 1)
        common["l3f_b"] = _col128(inputs["L3_ffn_bias"])

    if "l1f" in blocks:
        common["l1f_wg"] = _prep_w(np.ascontiguousarray(
            np.asarray(inputs["L1_ffn_w_gate"], dtype=np.float32).T))
        common["l1f_wu"] = _prep_w(np.ascontiguousarray(
            np.asarray(inputs["L1_ffn_w_up"], dtype=np.float32).T))
        common["l1f_wd"] = _prep_w(np.ascontiguousarray(
            np.asarray(inputs["L1_ffn_w_down"], dtype=np.float32).T))

    if "l2f" in blocks:
        common["l2f_wg"] = _prep_w(np.ascontiguousarray(
            np.asarray(inputs["L2_ffn_w_gate"], dtype=np.float32).T))
        common["l2f_wu"] = _prep_w(np.ascontiguousarray(
            np.asarray(inputs["L2_ffn_w_up"], dtype=np.float32).T))
        common["l2f_wd"] = _prep_w(np.ascontiguousarray(
            np.asarray(inputs["L2_ffn_w_down"], dtype=np.float32).T))

    if "l3a" in blocks:
        common["l3a_inw"] = _prep_w(np.asarray(inputs["L3_attn_in_w"], dtype=np.float32))
        common["l3a_inb"] = _col128(inputs["L3_attn_in_b"])
        common["l3a_gb"] = np.concatenate(
            [_col128(inputs["L3_attn_norm_gamma"]), _col128(inputs["L3_attn_norm_beta"])],
            axis=1)
        dw = np.asarray(inputs["L3_attn_dw_w"], dtype=np.float32)
        common["l3a_dww"] = np.ascontiguousarray(
            dw.reshape(16, 128, 7).transpose(1, 0, 2).reshape(128, 112))
        common["l3a_dwb"] = _col128(inputs["L3_attn_dw_b"])
        common["l3a_ldec"] = _col128(inputs["L3_attn_log_decay"])
        common["l3a_mixw"] = _prep_w(np.asarray(inputs["L3_attn_mix_w"], dtype=np.float32))
        common["l3a_mixb"] = _col128(inputs["L3_attn_mix_b"])
        common["l3a_outw"] = _prep_w(np.asarray(inputs["L3_attn_out_w"], dtype=np.float32))
        common["l3a_outb"] = _col128(inputs["L3_attn_out_b"])

    if "l1a" in blocks:
        common["l1a_wq"] = _prep_w(np.asarray(inputs["L1_attn_W_q"], dtype=np.float32))
        common["l1a_wk"] = _prep_w(np.asarray(inputs["L1_attn_W_k"], dtype=np.float32))
        common["l1a_wv"] = _prep_w(np.asarray(inputs["L1_attn_W_v"], dtype=np.float32))
        common["l1a_wdn"] = _prep_w(np.asarray(inputs["L1_attn_W_down_kv"], dtype=np.float32))
        upk = np.asarray(inputs["L1_attn_W_up_k"], dtype=np.float32)
        upv = np.asarray(inputs["L1_attn_W_up_v"], dtype=np.float32)
        common["l1a_wupk"] = np.ascontiguousarray(
            upk.reshape(2, 128, 1024).transpose(1, 0, 2).reshape(128, 2048))
        common["l1a_wupv"] = np.ascontiguousarray(
            upv.reshape(2, 128, 1024).transpose(1, 0, 2).reshape(128, 2048))
        woh = np.asarray(inputs["L1_attn_W_o"], dtype=np.float32)
        common["l1a_wo"] = np.ascontiguousarray(
            woh.reshape(16, 64, 8, 128).transpose(2, 0, 1, 3))
        dec = np.zeros((28, 128, 512), dtype=np.float32)
        for n in range(28):
            off = 128 * (n - 15)
            j = np.arange(128)[:, None]
            i = np.arange(512)[None, :]
            dec[n] = (np.exp(-0.01 * np.abs(off + i - j).astype(np.float64)) / 32.0
                      ).astype(np.float32)
        common["l1a_dec"] = dec

    if "l2a" in blocks:
        common["l2a_wq"] = _prep_w(np.ascontiguousarray(
            np.asarray(inputs["L2_attn_Wq"], dtype=np.float32).T))
        common["l2a_wk"] = _prep_w(np.ascontiguousarray(
            np.asarray(inputs["L2_attn_Wk"], dtype=np.float32).T))
        common["l2a_wv"] = _prep_w(np.ascontiguousarray(
            np.asarray(inputs["L2_attn_Wv"], dtype=np.float32).T))
        woh = np.ascontiguousarray(np.asarray(inputs["L2_attn_Wo"], dtype=np.float32).T)
        common["l2a_wo"] = np.ascontiguousarray(
            woh.reshape(16, 64, 8, 128).transpose(2, 0, 1, 3))
        inv = 1.0 / 10000.0 ** (np.arange(0, 64, 2, dtype=np.float64) / 64)
        freqs = np.outer(np.arange(S, dtype=np.float64), inv)
        emb = np.concatenate([freqs, freqs], axis=-1)
        cos64 = np.cos(emb).astype(np.float32).T
        sin64 = np.sin(emb).astype(np.float32).T
        sgn = np.ones((64, 1), dtype=np.float32)
        sgn[:32] = -1.0
        cosf = np.concatenate([cos64, cos64], axis=0)
        sinf = np.concatenate([sin64 * 0 + sin64, sin64], axis=0)
        common["l2a_rope"] = np.ascontiguousarray(cosf)
        common["l2a_ropes"] = np.ascontiguousarray(sinf)
        pm = np.zeros((128, 128), dtype=np.float32)
        for blk in range(2):
            for dd in range(32):
                pm[blk * 64 + dd + 32, blk * 64 + dd] = -1.0
                pm[blk * 64 + dd, blk * 64 + dd + 32] = 1.0
        common["l2a_perm"] = pm
        slopes = [2.0 ** (-8.0 / 16 * (i + 1)) for i in range(16)]
        mk = np.zeros((128, 16 * 128), dtype=np.float32)
        t_ = np.arange(128, dtype=np.float64)
        for h in range(16):
            m = np.where(t_[:, None] >= t_[None, :],
                         np.exp(-slopes[h] * (t_[:, None] - t_[None, :])), 0.0)
            mk[:, h * 128:(h + 1) * 128] = m.T.astype(np.float32)
        common["l2a_mask"] = mk
        qd = np.zeros((64, 16 * 128), dtype=np.float32)
        kd = np.zeros((128, 16), dtype=np.float32)
        for h in range(16):
            qd[:, h * 128:(h + 1) * 128] = np.exp(
                -slopes[h] * (t_[None, :] + 1.0)).astype(np.float32)
            kd[:, h] = np.exp(-slopes[h] * (127.0 - t_)).astype(np.float32)
        common["l2a_qdec"] = qd
        common["l2a_kdec"] = kd

    if "l0a" in blocks:
        ipw = np.asarray(inputs["L0_attn_in_proj_w"], dtype=np.float32)
        ipw = np.concatenate([ipw, np.zeros((1024, 96), dtype=np.float32)], axis=1)
        common["l0a_inw"] = _prep_w(ipw)
        common["l0a_lnb"] = np.concatenate(
            [_col128(inputs["L0_attn_ln_w"]), _col128(inputs["L0_attn_ln_b"])], axis=1)
        cw = np.asarray(inputs["L0_attn_conv_w"], dtype=np.float32)
        common["l0a_convw"] = np.ascontiguousarray(
            cw.reshape(18, 128, 4).transpose(1, 0, 2).reshape(128, 72))
        common["l0a_convb"] = _col128(inputs["L0_attn_conv_b"])
        dtb = np.zeros((128, 1), dtype=np.float32)
        dtb[0:32, 0] = np.asarray(inputs["L0_attn_dt_bias"], dtype=np.float32)
        common["l0a_dtb"] = dtb
        alg = np.zeros((128, 1), dtype=np.float32)
        alg[0:32, 0] = np.asarray(inputs["L0_attn_A_log"], dtype=np.float32)
        common["l0a_alog"] = alg
        common["l0a_dsk"] = np.asarray(inputs["L0_attn_D_skip"],
                                       dtype=np.float32).reshape(1, 32)
        common["l0a_rmsw"] = _col128(inputs["L0_attn_rms_w"])
        common["l0a_outw"] = _prep_w(np.asarray(inputs["L0_attn_out_proj_w"],
                                                dtype=np.float32))
        common["l0a_w1"] = _prep_w(np.asarray(inputs["L0_attn_mlp_w1"], dtype=np.float32))
        common["l0a_b1"] = _col128(inputs["L0_attn_mlp_b1"])
        common["l0a_w2"] = _prep_w(np.asarray(inputs["L0_attn_mlp_w2"], dtype=np.float32))
        common["l0a_b2"] = _col128(inputs["L0_attn_mlp_b2"])
        ut = np.zeros((128, 128), dtype=np.float32)
        for s_ in range(128):
            ut[s_, s_:] = 1.0
        common["l0a_utri"] = ut
        sel = np.zeros((128, 1), dtype=np.float32)
        sel[127, 0] = 1.0
        common["l0a_sel"] = sel
        rs = np.zeros((32, 32, 128), dtype=np.float32)
        for h in range(32):
            rs[h, h, :] = 1.0
        common["l0a_rsel"] = np.ascontiguousarray(rs.transpose(1, 0, 2).reshape(32, 32 * 128))

    if "l0f" in blocks:
        gwt = np.asarray(inputs["L0_ffn_gate_w"], dtype=np.float32).T
        common["l0f_gw"] = np.ascontiguousarray(
            gwt.reshape(8, 128, 8).transpose(1, 0, 2).reshape(128, 64))
        gp = np.asarray(inputs["L0_ffn_gate_proj_w"], dtype=np.float32)
        up = np.asarray(inputs["L0_ffn_up_proj_w"], dtype=np.float32)
        dn = np.asarray(inputs["L0_ffn_down_proj_w"], dtype=np.float32)
        common["l0f_wg"] = np.stack([_prep_w(np.ascontiguousarray(gp[e].T))
                                     for e in range(8)])
        common["l0f_wu"] = np.stack([_prep_w(np.ascontiguousarray(up[e].T))
                                     for e in range(8)])
        common["l0f_wd"] = np.stack([_prep_w(np.ascontiguousarray(dn[e].T))
                                     for e in range(8)])

    x = np.asarray(inputs["L0_attn_x"], dtype=np.float32)
    in_maps = []
    for c in range(NCORES):
        xc = x[c * BPC:(c + 1) * BPC].reshape(T, D).T
        m = dict(common)
        m["xin"] = np.ascontiguousarray(xc)
        if "l0f" in blocks:
            pf = np.zeros((128, 16), dtype=np.float32)
            for cc2 in range(c):
                for r in range(16):
                    pf[cc2 * 16 + r, r] = 1.0
            m["l0f_pfx"] = pf
        in_maps.append(m)
    return in_maps


def assemble_out(results):
    out = np.empty((B, S, D), dtype=np.float32)
    for c in range(NCORES):
        xc = results[c]["xout"]
        out[c * BPC:(c + 1) * BPC] = xc.T.reshape(BPC, S, D)
    return out


def kernel(**inputs):
    in_maps = prep_in_maps(inputs, ALL_BLOCKS)
    nc = build_program(ALL_BLOCKS)
    res = run_bass_kernel_spmd(nc, in_maps, core_ids=list(range(NCORES)))
    return assemble_out(res.results)
